# version 08
# baseline (speedup 1.0000x reference)
import sys
import numpy as np
from contextlib import ExitStack

sys.path.insert(0, "/opt/trn_rl_repo")
import concourse.bass as bass
import concourse.mybir as mybir
from concourse.bass_utils import run_bass_kernel_spmd
from concourse.bass import IndirectOffsetOnAxis

F32 = mybir.dt.float32
BF16 = mybir.dt.bfloat16
AF = mybir.ActivationFunctionType
ALU = mybir.AluOpType
AX = mybir.AxisListType

D = 1024
NH = 16
DH = 64
NE = 8
FF = 4096
NCH = 32
OWN0 = 16
CH = 128
NK = D // 128
NFB = 8
NFC = 4
TT = 512
CAP = 768
SPARSE = True
PHASES = "AB"

ENGINES = ("sync", "gpsimd", "tensor", "vector", "scalar")
_ENGSEM = {"vector": "sv", "scalar": "sa"}
_PSUM = {"psA", "psB", "psC", "psD", "psW", "psV", "psT", "psU"}


class Sched:

    def __init__(self, sems):
        self.q = {e: [] for e in ENGINES}
        self.cnt = {}
        self.waited = {e: {} for e in ENGINES}
        self.sems = sems
        self.dirty = {}
        self.readers = {}
        self.psum_ready = {}
        self.pe_pending_w = set()
        self.pe_pending_r = set()

    def _mark(self, table, nm, sem, c):
        d = table.setdefault(nm, {})
        if d.get(sem, -1) < c:
            d[sem] = c

    def op(self, eng, method, *args, inc=None, inc_n=1, **kw):
        aps = [a for a in args if hasattr(a, "tensor")]
        for v in kw.values():
            if hasattr(v, "tensor"):
                aps.append(v)
            elif hasattr(v, "ap") and hasattr(v.ap, "tensor"):
                aps.append(v.ap)
        outa = kw.get("out")
        if outa is None and args and hasattr(args[0], "tensor"):
            outa = args[0]
        names = [(a.tensor.name, a is outa) for a in aps]
        for nm, _ in names:
            for sem, c in self.dirty.get(nm, {}).items():
                self.wait(eng, sem, c)
            if nm in _PSUM and eng != "tensor":
                pr = self.psum_ready.get(nm)
                if pr is not None:
                    self.wait(eng, "sp", pr)
        for nm, is_out in names:
            if is_out:
                assert nm not in self.pe_pending_r or eng == "tensor", (
                    f"write to {nm} while a PE read is not yet fenced")
                for sem, c in self.readers.get(nm, {}).items():
                    self.wait(eng, sem, c)
        if inc is not None:
            self.cnt[inc] = self.cnt.get(inc, 0) + inc_n
            sem, n = self.sems[inc], inc_n

            def fn(e, _m=method, _a=args, _k=kw, _s=sem, _n=n):
                getattr(e, _m)(*_a, **_k).then_inc(_s, _n)
        else:
            def fn(e, _m=method, _a=args, _k=kw):
                getattr(e, _m)(*_a, **_k)
        self.q[eng].append(fn)
        if eng == "tensor":
            for nm, is_out in names:
                if is_out:
                    self.pe_pending_w.add(nm)
                else:
                    self.pe_pending_r.add(nm)
            if inc is not None:
                c = self.cnt[inc]
                for nm in self.pe_pending_w:
                    self.psum_ready[nm] = c
                self.pe_pending_w.clear()
                for nm in self.pe_pending_r:
                    self._mark(self.readers, nm, "sp", c)
                self.pe_pending_r.clear()
        elif eng in _ENGSEM:
            assert inc == _ENGSEM[eng], f"writer on {eng} must inc: {method}"
            c = self.cnt[inc]
            for nm, is_out in names:
                if is_out:
                    self._mark(self.dirty, nm, _ENGSEM[eng], c)
                else:
                    self._mark(self.readers, nm, _ENGSEM[eng], c)
        elif inc is not None:
            c = self.cnt[inc]
            for nm, is_out in names:
                if not is_out:
                    self._mark(self.readers, nm, inc, c)
        return self.cnt.get(inc, 0) if inc else None

    def wait(self, eng, sem, val=None):
        if val is None:
            val = self.cnt.get(sem, 0)
        if val <= 0 or self.waited[eng].get(sem, -1) >= val:
            return
        self.waited[eng][sem] = val
        h = self.sems[sem]
        self.q[eng].append(lambda e, _h=h, _v=val: e.wait_ge(_h, _v))


def build_program():
    nc = bass.Bass("TRN2", target_bir_lowering=False)
    n_own = NCH - OWN0
    TT_ = min(TT, n_own * CH)
    NTT = (n_own * CH) // TT_
    ntok = TT_ // 128

    xpad = nc.dram_tensor("xpad", [NCH * CH, D], F32, kind="ExternalInput").ap()
    wq = nc.dram_tensor("wq", [D, D], F32, kind="ExternalInput").ap()
    wk = nc.dram_tensor("wk", [D, D], F32, kind="ExternalInput").ap()
    wv = nc.dram_tensor("wv", [D, D], F32, kind="ExternalInput").ap()
    wout = nc.dram_tensor("wout", [D, D], F32, kind="ExternalInput").ap()
    wbeta = nc.dram_tensor("wbeta", [D, NH], F32, kind="ExternalInput").ap()
    bbeta = nc.dram_tensor("bbeta", [NH], F32, kind="ExternalInput").ap()
    wgate = nc.dram_tensor("wgate", [D, NE], F32, kind="ExternalInput").ap()
    wgu = nc.dram_tensor("wgu", [NE, D, 2 * FF], F32, kind="ExternalInput").ap()
    wdn = nc.dram_tensor("wdn", [NE, FF, D], F32, kind="ExternalInput").ap()
    cident = nc.dram_tensor("cident", [128, 128], F32, kind="ExternalInput").ap()
    cslmask = nc.dram_tensor("cslmask", [128, 128], F32, kind="ExternalInput").ap()
    csumask = nc.dram_tensor("csumask", [128, 128], F32, kind="ExternalInput").ap()
    cexp = nc.dram_tensor("cexp", [128, NE], F32, kind="ExternalInput").ap()
    xg = nc.dram_tensor("xg", [NE * CAP, D], BF16, kind="Internal").ap()
    yout = nc.dram_tensor("yout", [NE * CAP, D], BF16, kind="Internal").ap()
    out = nc.dram_tensor("out", [n_own * CH, D], F32, kind="ExternalOutput").ap()

    with ExitStack() as ctx:
        sb = lambda n, s, d: ctx.enter_context(nc.sbuf_tensor(n, s, d))
        psd = lambda n, s, d: ctx.enter_context(nc.psum_tensor(n, s, d))

        SHARED = 106816
        shared = sb("shared", [128, SHARED], mybir.dt.uint8)
        shared_base = nc.lookup_mloc(shared).addr
        _roff = [0]

        def _ralloc(name, shape, dtype):
            per = 1
            for d_ in shape[1:]:
                per *= d_
            per *= mybir.dt.size(dtype)
            h = nc.alloc_sbuf_tensor_at(name, shape, dtype,
                                        offset=shared_base + _roff[0])
            _roff[0] = (_roff[0] + per + 31) // 32 * 32
            assert _roff[0] <= SHARED, f"shared region overflow at {name}"
            return h

        xnT_own = sb("xnT_own", [128, NK * n_own * CH], BF16)
        acc = sb("acc", [128, n_own * D], F32)
        gates = None if SPARSE else sb("gates", [128, n_own * NE], F32)
        M32 = sb("M32", [64, NH * DH], F32)
        M16 = sb("M16", [64, NH * DH], BF16)
        idxs = sb("idxs", [128, 2 * (NCH - OWN0)], mybir.dt.uint32)
        groutes = sb("groutes", [128, 2 * (NCH - OWN0)], F32)
        identB = sb("identB", [128, 128], BF16)
        identF = sb("identF", [16, 16], F32)
        slmask = sb("slmask", [128, 128], BF16)
        sumask = sb("sumask", [128, 128], BF16)

        WqB = _ralloc("WqB", [128, NK * D], BF16)
        WkB = _ralloc("WkB", [128, NK * D], BF16)
        WvB = _ralloc("WvB", [128, NK * D], BF16)
        WoB = _ralloc("WoB", [128, NK * D], BF16)
        WbB = _ralloc("WbB", [128, NK * NH], BF16)
        WgB = _ralloc("WgB", [128, NK * NE], BF16)
        bbn = _ralloc("bbn", [16, 1], F32)
        bbp = _ralloc("bbp", [16, 1], F32)
        xin = _ralloc("xin0", [128, D], F32)
        xnb = _ralloc("xnb", [128, D], BF16)
        KT = _ralloc("KTb", [128, 8 * 128], BF16)
        VT = _ralloc("VTb", [128, 8 * 128], BF16)
        QT = _ralloc("QTb", [128, 8 * 128], BF16)
        betaT = _ralloc("betaT", [16, 128], F32)
        bcols = _ralloc("bcols", [128, 16], F32)
        bcolsn = _ralloc("bcolsn", [128, 16], F32)
        zgs = _ralloc("zgs", [8, 128], F32)
        zt = _ralloc("ztb", [128, 8], F32)
        Ktok = [_ralloc(f"Ktok{g}", [128, 8 * DH], BF16) for g in range(2)]
        Kn = Ktok
        Vtb = [_ralloc(f"Vtb{g}", [128, 8 * DH], BF16) for g in range(2)]
        Rt = Vtb
        KnT = [_ralloc(f"KnT{g}", [64, 8 * 128], BF16) for g in range(2)]
        ssb = [_ralloc(f"ssb{g}", [128, 8], F32) for g in range(2)]
        invb = [_ralloc(f"invb{g}", [128, 8], F32) for g in range(2)]
        Gb = [_ralloc(f"Gb{g}", [128, 8 * 128], BF16) for g in range(2)]
        Qh0 = Gb
        Gp = [_ralloc(f"Gp{g}", [128, 8 * 128], BF16) for g in range(2)]
        Pk = Gb
        Pq = Gp
        Sa = [_ralloc(f"Sa{g}", [128, 8 * 128], BF16) for g in range(2)]
        Xg = [_ralloc(f"Xg{g}", [128, 8 * DH], BF16) for g in range(2)]
        Lg = Sa
        Oall = [_ralloc(f"Oall{g}", [128, 8 * DH], BF16) for g in range(2)]
        Qtok = Oall
        OTb = _ralloc("OTb", [128, 8 * 128], BF16)
        c_s1 = _ralloc("c_s1", [128, 1], F32)
        c_mean = _ralloc("c_mean", [128, 1], F32)
        c_s2 = _ralloc("c_s2", [128, 1], F32)
        c_m2 = _ralloc("c_m2", [128, 1], F32)
        c_var = _ralloc("c_var", [128, 1], F32)
        c_rstd = _ralloc("c_rstd", [128, 1], F32)
        g_nm = _ralloc("g_nm", [128, 1], F32)
        g_ex = _ralloc("g_ex", [128, 8], F32)
        g_sum = _ralloc("g_sum", [128, 1], F32)
        g_rs = _ralloc("g_rs", [128, 1], F32)
        g_p = _ralloc("g_p", [128, 8], F32)
        g_m1 = _ralloc("g_m1", [128, 1], F32)
        g_m2 = _ralloc("g_m2", [128, 1], F32)
        g_mv = _ralloc("g_mv", [128, 8], F32)
        g_pm1 = _ralloc("g_pm1", [128, 8], F32)
        g_pm2 = _ralloc("g_pm2", [128, 8], F32)
        g_ne = _ralloc("g_ne", [128, 8], F32)
        g_den = _ralloc("g_den", [128, 1], F32)
        r_cnt8 = sb("r_cnt8", [8, 1], F32)
        r_tmp8 = sb("r_tmp8", [8, 1], F32)
        r_snzb = sb("r_snzb", [128, 8], BF16)
        r_m1 = sb("r_m1", [128, 8], F32)
        r_m2 = sb("r_m2", [128, 8], F32)
        r_t = sb("r_t", [128, 8], F32)
        r_o1 = sb("r_o1", [128, 1], F32)
        r_o2 = sb("r_o2", [128, 1], F32)
        capexp = sb("capexp", [128, NE], F32)
        A_BYTES = _roff[0]
        import os
        if os.environ.get("PRINT_AB"): print("A_BYTES", A_BYTES)

        _roff[0] = 0
        WguB = [_ralloc("WguB0", [128, NK * 1024], BF16),
                _ralloc("WguB1", [128, NK * 1024], BF16)]
        WdnB = [_ralloc("WdnB0", [128, NFC * 1024], BF16),
                _ralloc("WdnB1", [128, NFC * 1024], BF16)]
        actb = [_ralloc(f"actb{i}", [128, 512], BF16) for i in range(NFC)]
        sgb = [_ralloc("sgb0", [128, 256], F32), _ralloc("sgb1", [128, 256], F32)]
        NBL = CAP // 128
        xgl = [_ralloc("xgl0", [128, NBL * D], BF16),
               _ralloc("xgl1", [128, NBL * D], BF16)]
        xgT = _ralloc("xgT", [128, NK * CAP], BF16)
        yacc = [_ralloc(f"yacc{i}", [128, (NBL // 2) * 512], BF16) for i in range(4)]
        ybk = [xgl[0][:, 0:D], xgl[0][:, D:2 * D]]
        if os.environ.get("PRINT_AB"): print("B_BYTES", _roff[0])

        psA = psd("psA", [128, 512], F32)
        psB = psd("psB", [128, 512], F32)
        psC = psd("psC", [128, 512], F32)
        psD = psd("psD", [128, 512], F32)
        psW = psd("psW", [128, 512], F32)
        psV = psd("psV", [128, 512], F32)
        psT = psd("psT", [128, 1024], BF16)
        psU = psd("psU", [128, 1024], BF16)
        fpair = [(psA, psB), (psC, psD)]
        trb = [psT, psU]
        kb = [psA, psC]

        sems = {}
        for name in ("sc", "si", "sxe", "sxo", "swp", "sw", "so",
                     "sg", "sy", "sxl", "sxm", "sp", "sv", "sa"):
            sems[name] = ctx.enter_context(nc.semaphore(name))
        S = Sched(sems)
        block = ctx.enter_context(nc.Block())
        op, wait = S.op, S.wait

        def vop(method, *args, **kw):
            op("vector", method, *args, inc="sv", **kw)

        def aop(method, *args, **kw):
            op("scalar", method, *args, inc="sa", **kw)

        def F32T(g, hl):
            bank = fpair[g][0] if hl < 4 else fpair[g][1]
            s_ = hl % 4
            return bank[:, s_ * 128:(s_ + 1) * 128]

        op("sync", "dma_start", out=identF[:], in_=cident[0:16, 0:16], inc="sc", inc_n=16)
        op("gpsimd", "dma_start", out=slmask[:], in_=cslmask, inc="swp", inc_n=16)
        op("gpsimd", "dma_start", out=sumask[:], in_=csumask, inc="swp", inc_n=16)
        op("sync", "dma_start", out=bbp[:],
           in_=bbeta.rearrange("(p o) -> p o", o=1), inc="sc", inc_n=16)
        op("sync", "dma_start",
           out=acc[:].rearrange("p (i d) -> p i d", d=D),
           in_=xpad[OWN0 * CH:NCH * CH, :].rearrange("(i p) d -> p i d", p=128),
           inc="si", inc_n=16)

        op("gpsimd", "dma_start", out=identB[:], in_=cident, inc="swp", inc_n=16)
        for wsrc, wdst in ((wq, WqB), (wk, WkB), (wv, WvB), (wout, WoB)):
            for k in range(NK):
                op("gpsimd", "dma_start", out=wdst[:, k * D:(k + 1) * D],
                   in_=wsrc[k * 128:(k + 1) * 128, :], inc="swp", inc_n=16)
        op("gpsimd", "dma_start",
           out=WbB[:].rearrange("p (k h) -> p k h", k=NK),
           in_=wbeta.rearrange("(k p) h -> p k h", p=128), inc="swp", inc_n=16)
        op("gpsimd", "dma_start",
           out=WgB[:].rearrange("p (k h) -> p k h", k=NK),
           in_=wgate.rearrange("(k p) h -> p k h", p=128), inc="swp", inc_n=16)
        proj_w = S.cnt["swp"]

        op("sync", "dma_start", out=capexp[:], in_=cexp, inc="sc", inc_n=16)
        wait("vector", "sc")
        if SPARSE:
            vop("memset", xnb[:], 0.0)
            wait("gpsimd", "sv")
            for b in range(NE * CAP // 128):
                op("gpsimd", "dma_start",
                   out=xg[b * 128:(b + 1) * 128, :], in_=xnb[:],
                   inc="sg", inc_n=16)
        vop("memset", r_cnt8[:], 0.0)
        vop("tensor_scalar_mul", bbn[:], bbp[:], -1.0)
        vop("memset", M32[:], 0.0)
        vop("memset", M16[:], 0.0)

        wait("tensor", "sc")
        wait("tensor", "swp", proj_w)
        def emit_ln(c):
            sxp = "sxe" if c % 2 == 0 else "sxo"
            op("sync", "dma_start", out=xin[:], in_=xpad[c * CH:(c + 1) * CH, :],
               inc=sxp, inc_n=16)
            wait("vector", sxp)
            wait("vector", "sg")
            vop("reduce_sum", c_s1[:], xin[:], axis=AX.X)
            vop("tensor_scalar_mul", c_mean[:], c_s1[:], 1.0 / D)
            vop("tensor_mul", xnb[:], xin[:], xin[:])
            vop("reduce_sum", c_s2[:], xnb[:], axis=AX.X)
            vop("tensor_mul", c_m2[:], c_mean[:], c_mean[:])
            vop("scalar_tensor_tensor", c_var[:], c_s2[:], 1.0 / D, c_m2[:],
                ALU.mult, ALU.subtract)
            vop("tensor_scalar_add", c_var[:], c_var[:], 1e-5)
            aop("activation", c_rstd[:], c_var[:], AF.Ln)
            aop("activation", c_rstd[:], c_rstd[:], AF.Exp, scale=-0.5)
            vop("tensor_scalar", xnb[:], xin[:], c_mean[:], c_rstd[:],
                ALU.subtract, ALU.mult)

        def front_pieces(c):
            own = c >= OWN0
            tc = c - OWN0
            slot = tc if own else (c & 1)

            def XN(k, _slot=slot):
                base = k * (n_own * CH) + _slot * CH
                return xnT_own[:, base:base + CH]

            def p_xnt():
                for i in range(NK):
                    op("tensor", "transpose", psT[:, i * 128:(i + 1) * 128],
                       in_=xnb[:, i * 128:(i + 1) * 128], identity=identB[:],
                       inc=("sp" if i == NK - 1 else None))
                xd = xnT_own[:].rearrange("p (k t) -> p k t", k=NK)[
                    :, :, slot * CH:(slot + 1) * CH]
                aop("copy", xd, psT[:].rearrange("p (k t) -> p k t", k=NK))

            def mk_proj(wbuf, dst, half, bank):
                def p():
                    for i in range(4):
                        ii = half * 4 + i
                        for k in range(NK):
                            op("tensor", "matmul", bank[:, i * 128:(i + 1) * 128],
                               lhsT=wbuf[:, k * D + ii * 128:k * D + (ii + 1) * 128],
                               rhs=XN(k), start=(k == 0), stop=(k == NK - 1),
                               inc=("sp" if (i == 3 and k == NK - 1) else None))
                    aop("copy", dst[:, half * 512:(half + 1) * 512], bank[:])
                return p

            def p_beta_mm():
                for k in range(NK):
                    op("tensor", "matmul", psV[0:16, 0:128],
                       lhsT=WbB[:, k * NH:(k + 1) * NH], rhs=XN(k),
                       start=(k == 0), stop=(k == NK - 1),
                       inc=("sp" if k == NK - 1 else None))
                aop("activation", betaT[:], psV[0:16, 0:128], AF.Exp,
                    scale=-1.0, bias=bbn[:])
                vop("tensor_scalar_add", betaT[:], betaT[:], 1.0)
                vop("reciprocal", betaT[:], betaT[:])

            def p_beta_fin():
                op("tensor", "transpose", psV[:, 288:304], in_=betaT[:],
                   identity=identF[:], inc="sp")
                vop("tensor_copy", bcols[:], psV[:, 288:304])
                vop("tensor_scalar_mul", bcolsn[:], bcols[:], -1.0)

            def p_gate_mm():
                for k in range(NK):
                    op("tensor", "matmul", psV[0:8, 128:256],
                       lhsT=WgB[:, k * NE:(k + 1) * NE], rhs=XN(k),
                       start=(k == 0), stop=(k == NK - 1),
                       inc=("sp" if k == NK - 1 else None))
                vop("tensor_copy", zgs[:], psV[0:8, 128:256])

            def p_gate_fin():
                op("tensor", "transpose", psV[:, 304:312], in_=zgs[:],
                   identity=identF[0:8, 0:8], inc="sp")
                vop("tensor_copy", zt[:], psV[:, 304:312])
                vop("reduce_max", g_nm[:], zt[:], axis=AX.X, negate=True)
                aop("activation", g_ex[:], zt[:], AF.Exp, bias=g_nm[:])
                vop("reduce_sum", g_sum[:], g_ex[:], axis=AX.X)
                vop("reciprocal", g_rs[:], g_sum[:])
                vop("tensor_scalar_mul", g_p[:], g_ex[:], g_rs[:])
                vop("reduce_max", g_m1[:], g_p[:], axis=AX.X)
                vop("memset", g_mv[:], -1e30)
                vop("tensor_copy", g_mv[:, 0:1], g_m1[:])
                vop("match_replace", g_pm1[:], in_to_replace=g_mv[:],
                    in_values=g_p[:], imm_value=-1.0)
                vop("reduce_max", g_m2[:], g_pm1[:], axis=AX.X)
                vop("tensor_copy", g_mv[:, 0:1], g_m2[:])
                vop("match_replace", g_pm2[:], in_to_replace=g_mv[:],
                    in_values=g_pm1[:], imm_value=-1.0)
                vop("tensor_tensor", g_ne[:], g_p[:], g_pm2[:], ALU.not_equal)
                vop("tensor_add", g_den[:], g_m1[:], g_m2[:])
                vop("tensor_scalar_max", g_den[:], g_den[:], 1e-9)
                vop("reciprocal", g_den[:], g_den[:])
                if not SPARSE:
                    vop("tensor_mul", g_ne[:], g_ne[:], g_p[:])
                    vop("tensor_scalar", gates[:, tc * NE:(tc + 1) * NE],
                        g_ne[:], g_den[:], None, ALU.mult)

            def p_route():
                vop("tensor_tensor", r_m1[:], g_p[:], g_pm1[:], ALU.not_equal)
                vop("tensor_tensor", r_m2[:], g_pm1[:], g_pm2[:], ALU.not_equal)
                vop("tensor_add", r_t[:], r_m1[:], r_m2[:])
                vop("tensor_copy", r_snzb[:], r_t[:])

            def p_route_b():
                op("tensor", "transpose", psT[0:8, 0:128], in_=r_snzb[:],
                   identity=identB[:], inc="sp")
                vop("tensor_copy", zgs[:], psT[0:8, 0:128])
                vop("tensor_tensor_scan", betaT[0:8, :], zgs[:],
                    zgs[:], 0.0, ALU.add, ALU.max)
                vop("tensor_scalar", betaT[0:8, :], betaT[0:8, :], r_cnt8[:], -1.0,
                    ALU.add, ALU.add)
                vop("reduce_sum", r_tmp8[:], zgs[:], axis=AX.X)
                vop("tensor_add", r_cnt8[:], r_cnt8[:], r_tmp8[:])

            def p_route_c():
                op("tensor", "transpose", psV[:, 312:320], in_=betaT[0:8, :],
                   identity=identF[0:8, 0:8], inc="sp")
                vop("tensor_tensor", r_t[:], psV[:, 312:320], capexp[:], ALU.add)
                vop("tensor_mul", g_ex[:], r_t[:], r_m1[:])
                vop("reduce_sum", r_o1[:], g_ex[:], axis=AX.X)
                vop("tensor_mul", g_ex[:], r_t[:], r_m2[:])
                vop("reduce_sum", r_o2[:], g_ex[:], axis=AX.X)
                vop("tensor_scalar_min", r_o1[:], r_o1[:], float(NE * CAP - 1))
                vop("tensor_scalar_min", r_o2[:], r_o2[:], float(NE * CAP - 1))
                vop("tensor_copy", idxs[:, 2 * tc:2 * tc + 1], r_o1[:])
                vop("tensor_copy", idxs[:, 2 * tc + 1:2 * tc + 2], r_o2[:])
                vop("tensor_mul", groutes[:, 2 * tc:2 * tc + 1], g_m1[:], g_den[:])
                vop("tensor_mul", groutes[:, 2 * tc + 1:2 * tc + 2], g_m2[:], g_den[:])
                op("gpsimd", "indirect_dma_start", out=xg,
                   out_offset=IndirectOffsetOnAxis(idxs[:, 2 * tc:2 * tc + 1], 0),
                   in_=xnb[:], in_offset=None, inc="sg", inc_n=16)
                op("gpsimd", "indirect_dma_start", out=xg,
                   out_offset=IndirectOffsetOnAxis(idxs[:, 2 * tc + 1:2 * tc + 2], 0),
                   in_=xnb[:], in_offset=None, inc="sg", inc_n=16)

            pieces = [p_xnt,
                      mk_proj(WkB, KT, 0, psW), mk_proj(WkB, KT, 1, psV),
                      mk_proj(WvB, VT, 0, psW), mk_proj(WvB, VT, 1, psV)]
            if own:
                pieces += [mk_proj(WqB, QT, 0, psW), mk_proj(WqB, QT, 1, psV)]
            pieces += [p_beta_mm, p_beta_fin]
            if own:
                pieces += [p_gate_mm, p_gate_fin]
                if SPARSE:
                    pieces += [p_route, p_route_b, p_route_c]
            return pieces

        def chunk_steps(c):
            own = c >= OWN0
            tc = c - OWN0

            def PTile(buf, g, p):
                t = g * 4 + p
                return buf[:, t * 128:(t + 1) * 128]

            def BC(g, hl):
                j = g * 8 + hl
                return bcols[:, j:j + 1]

            def BCN(g, hl):
                j = g * 8 + hl
                return bcolsn[:, j:j + 1]

            def cop(g, *args):
                if g == 0:
                    vop("tensor_copy", *args)
                else:
                    aop("copy", *args)

            def s_t1(g):
                tr = trb[g]
                for p in range(4):
                    op("tensor", "transpose", tr[:, p * 128:(p + 1) * 128],
                       in_=PTile(KT, g, p), identity=identB[:],
                       inc=("sp" if p == 3 else None))
                cop(g, Ktok[g][:], tr[:, 0:512])

            def s_t5a(g):
                tr = trb[g]
                for p in range(4):
                    op("tensor", "transpose", tr[:, p * 128:(p + 1) * 128],
                       in_=PTile(VT, g, p), identity=identB[:],
                       inc=("sp" if p == 3 else None))
                for hl in range(8):
                    pp, b = hl // 2, 64 * (hl % 2)
                    if g == 0:
                        vop("tensor_scalar_mul", Vtb[g][:, hl * DH:(hl + 1) * DH],
                            tr[:, pp * 128 + b:pp * 128 + b + 64], BC(g, hl))
                    else:
                        aop("activation", Vtb[g][:, hl * DH:(hl + 1) * DH],
                            tr[:, pp * 128 + b:pp * 128 + b + 64], AF.Copy,
                            scale=BC(g, hl))

            def s_q1(g):
                tr = trb[g]
                for p in range(4):
                    op("tensor", "transpose", tr[:, p * 128:(p + 1) * 128],
                       in_=PTile(QT, g, p), identity=identB[:],
                       inc=("sp" if p == 3 else None))
                cop(g, Qtok[g][:], tr[:, 0:512])

            def s_t2(g):
                vop("tensor_mul", kb[g][:], Ktok[g][:], Ktok[g][:])
                vop("reduce_sum", ssb[g][:],
                    kb[g][:].rearrange("p (h d) -> p h d", h=8), axis=AX.X)
                vop("tensor_scalar_max", ssb[g][:], ssb[g][:], 1e-24)
                aop("activation", invb[g][:], ssb[g][:], AF.Ln)
                aop("activation", invb[g][:], invb[g][:], AF.Exp, scale=-0.5)
                for hl in range(8):
                    vop("tensor_scalar_mul", Kn[g][:, hl * DH:(hl + 1) * DH],
                        Ktok[g][:, hl * DH:(hl + 1) * DH], invb[g][:, hl:hl + 1])

            def s_t4(g):
                tr = trb[g]
                for hl in range(8):
                    op("tensor", "transpose", tr[0:64, hl * 128:(hl + 1) * 128],
                       in_=Kn[g][:, hl * DH:(hl + 1) * DH], identity=identB[:],
                       inc=("sp" if hl == 7 else None))
                cop(g, KnT[g][:], tr[0:64, :])

            def s_t5b(g):
                for hl in range(8):
                    hg = g * 8 + hl
                    op("tensor", "matmul", kb[g][:, hl * DH:(hl + 1) * DH],
                       lhsT=KnT[g][0:64, hl * 128:(hl + 1) * 128],
                       rhs=M16[0:64, hg * DH:(hg + 1) * DH],
                       start=True, stop=True, inc=("sp" if hl == 7 else None))
                for hl in range(8):
                    vop("scalar_tensor_tensor", Rt[g][:, hl * DH:(hl + 1) * DH],
                        kb[g][:, hl * DH:(hl + 1) * DH], BCN(g, hl),
                        Vtb[g][:, hl * DH:(hl + 1) * DH], ALU.mult, ALU.add)

            def s_t6(g):
                for hl in range(8):
                    op("tensor", "matmul", F32T(g, hl),
                       lhsT=KnT[g][0:64, hl * 128:(hl + 1) * 128],
                       rhs=KnT[g][0:64, hl * 128:(hl + 1) * 128],
                       start=True, stop=True, inc=("sp" if hl == 7 else None))
                for hl in range(8):
                    vop("scalar_tensor_tensor", Gb[g][:, hl * 128:(hl + 1) * 128],
                        F32T(g, hl), BC(g, hl), slmask[:], ALU.mult, ALU.mult)

            def s_t7(g):
                tr = trb[g]
                for hl in range(8):
                    op("tensor", "transpose", tr[:, hl * 128:(hl + 1) * 128],
                       in_=Gb[g][:, hl * 128:(hl + 1) * 128], identity=identB[:],
                       inc=("sp" if hl == 7 else None))
                cop(g, Gp[g][:], tr[:])
                for hl in range(8):
                    vop("tensor_sub", Sa[g][:, hl * 128:(hl + 1) * 128],
                        identB[:], Gp[g][:, hl * 128:(hl + 1) * 128])

            def s_lev_p(g, lev):
                for hl in range(8):
                    sl_ = slice(hl * 128, (hl + 1) * 128)
                    op("tensor", "matmul", F32T(g, hl),
                       lhsT=Pq[g][:, sl_], rhs=Pk[g][:, sl_],
                       start=True, stop=True, inc=("sp" if hl == 7 else None))
                cop(g, Pk[g][:, 0:512], fpair[g][0][:])
                cop(g, Pk[g][:, 512:1024], fpair[g][1][:])

            def s_lev_pt(g, lev):
                tr = trb[g]
                for hl in range(8):
                    op("tensor", "transpose", tr[:, hl * 128:(hl + 1) * 128],
                       in_=Pk[g][:, hl * 128:(hl + 1) * 128], identity=identB[:],
                       inc=("sp" if hl == 7 else None))
                cop(g, Pq[g][:], tr[:])

            def s_lev_s(g, lev):
                for hl in range(8):
                    sl_ = slice(hl * 128, (hl + 1) * 128)
                    op("tensor", "matmul", F32T(g, hl), lhsT=Pk[g][:, sl_],
                       rhs=Sa[g][:, sl_], start=True, stop=False)
                    op("tensor", "matmul", F32T(g, hl), lhsT=identB[:],
                       rhs=Sa[g][:, sl_], start=False, stop=True,
                       inc=("sp" if hl == 7 else None))
                cop(g, Sa[g][:, 0:512], fpair[g][0][:])
                cop(g, Sa[g][:, 512:1024], fpair[g][1][:])

            def s_t9(g):
                for hl in range(8):
                    op("tensor", "matmul", kb[g][:, hl * DH:(hl + 1) * DH],
                       lhsT=Sa[g][:, hl * 128:(hl + 1) * 128],
                       rhs=Rt[g][:, hl * DH:(hl + 1) * DH],
                       start=True, stop=True, inc=("sp" if hl == 7 else None))
                cop(g, Xg[g][:], kb[g][:])

            def s_t10(g):
                for hl in range(8):
                    op("tensor", "matmul", kb[g][0:64, hl * DH:(hl + 1) * DH],
                       lhsT=Kn[g][:, hl * DH:(hl + 1) * DH],
                       rhs=Xg[g][:, hl * DH:(hl + 1) * DH],
                       start=True, stop=True, inc=("sp" if hl == 7 else None))
                for hl in range(8):
                    hg = g * 8 + hl
                    vop("tensor_tensor", M32[:, hg * DH:(hg + 1) * DH],
                        M32[:, hg * DH:(hg + 1) * DH],
                        kb[g][0:64, hl * DH:(hl + 1) * DH], ALU.add)

            def s_q2(g):
                tr = trb[g]
                for hl in range(8):
                    op("tensor", "transpose", tr[0:64, hl * 128:(hl + 1) * 128],
                       in_=Qtok[g][:, hl * DH:(hl + 1) * DH], identity=identB[:],
                       inc=("sp" if hl == 7 else None))
                cop(g, Qh0[g][0:64, :], tr[0:64, :])

            def s_kq(g):
                for hl in range(8):
                    op("tensor", "matmul", F32T(g, hl),
                       lhsT=KnT[g][0:64, hl * 128:(hl + 1) * 128],
                       rhs=Qh0[g][0:64, hl * 128:(hl + 1) * 128],
                       start=True, stop=True, inc=("sp" if hl == 7 else None))
                for hl in range(8):
                    vop("tensor_mul", Lg[g][:, hl * 128:(hl + 1) * 128],
                        F32T(g, hl), sumask[:])

            def s_o(g):
                for hl in range(8):
                    hg = g * 8 + hl
                    op("tensor", "matmul", kb[g][:, hl * DH:(hl + 1) * DH],
                       lhsT=Qh0[g][0:64, hl * 128:(hl + 1) * 128],
                       rhs=M16[0:64, hg * DH:(hg + 1) * DH],
                       start=True, stop=False)
                    op("tensor", "matmul", kb[g][:, hl * DH:(hl + 1) * DH],
                       lhsT=Lg[g][:, hl * 128:(hl + 1) * 128],
                       rhs=Xg[g][:, hl * DH:(hl + 1) * DH],
                       start=False, stop=True, inc=("sp" if hl == 7 else None))
                cop(g, Oall[g][:], kb[g][:])

            def s_ot(g):
                tr = trb[g]
                for p in range(4):
                    op("tensor", "transpose", tr[:, p * 128:(p + 1) * 128],
                       in_=Oall[g][:, p * 128:(p + 1) * 128], identity=identB[:],
                       inc=("sp" if p == 3 else None))
                cop(g, OTb[:, g * 512:(g + 1) * 512], tr[:, 0:512])

            base = [s_t1, s_t5a]
            if own:
                base.append(s_q1)
            base += [s_t2, s_t4, s_t5b, s_t6, s_t7]
            for lev in range(1, 7):
                base.append(lambda g, _l=lev: s_lev_p(g, _l))
                if lev < 6:
                    base.append(lambda g, _l=lev: s_lev_pt(g, _l))
                base.append(lambda g, _l=lev: s_lev_s(g, _l))
            base += [s_t9, s_t10]
            if own:
                base += [s_q2, s_kq, s_o, s_ot]
            units = []
            for st in base:
                units.append(lambda _s=st: _s(0))
                units.append(lambda _s=st: _s(1))
                if (own and st is s_o) or (not own and st is s_t10):
                    units.append(lambda: aop("copy", M16[:], M32[:]))

            if own:
                def p_wout():
                    for i in range(8):
                        bank = psC if i < 4 else psD
                        for k in range(NK):
                            op("tensor", "matmul",
                               bank[:, (i % 4) * 128:((i % 4) + 1) * 128],
                               lhsT=WoB[:, k * D + i * 128:k * D + (i + 1) * 128],
                               rhs=OTb[:, k * 128:(k + 1) * 128],
                               start=(k == 0), stop=(k == NK - 1),
                               inc=("sp" if (i == 7 and k == NK - 1) else None))
                    vop("tensor_copy", Lg[1][:, 0:512], psC[:])
                    vop("tensor_copy", Lg[1][:, 512:1024], psD[:])
                    for i in range(8):
                        op("tensor", "transpose", psT[:, i * 128:(i + 1) * 128],
                           in_=Lg[1][:, i * 128:(i + 1) * 128], identity=identB[:],
                           inc=("sp" if i == 7 else None))
                    wait("vector", "si", 16)
                    vop("tensor_tensor", acc[:, tc * D:(tc + 1) * D],
                        acc[:, tc * D:(tc + 1) * D], psT[:], ALU.add)
                units.append(p_wout)
            return units

        if "A" in PHASES:
            emit_ln(0)
            for p in front_pieces(0):
                p()
        for c in (range(NCH) if "A" in PHASES else []):
            units = chunk_steps(c)
            if c + 1 < NCH:
                nf = [lambda _c=c + 1: emit_ln(_c)] + front_pieces(c + 1)
            else:
                nf = []
            nu, nn = len(units), len(nf)
            k = 0
            for i, u in enumerate(units):
                u()
                while k < nn and (i + 1) * nn >= (k + 1) * nu:
                    nf[k]()
                    k += 1
            while k < nn:
                nf[k]()
                k += 1

        wait("tensor", "sv")
        wait("tensor", "sa")
        pa_sp = S.cnt.get("sp", 0)
        wait("gpsimd", "sp", pa_sp)
        wait("gpsimd", "sv")
        wait("gpsimd", "sa")
        wgu5 = wgu.rearrange("e r (h f c) -> e f r h c", h=2, f=NFB)
        wdn4 = wdn.rearrange("e (f p) d -> e p f d", p=128)

        wg_free = [pa_sp, pa_sp]
        it = 0
        for e_i in range(0 if (SPARSE or "B" not in PHASES) else NE):
            for fb in range(NFB):
                par = it % 2
                wait("gpsimd", "sp", wg_free[par])
                wait("gpsimd", "sw")
                for k in range(NK):
                    op("gpsimd", "dma_start",
                       out=WguB[par][:, k * 1024:(k + 1) * 1024].rearrange(
                           "p (h c) -> p h c", h=2),
                       in_=wgu5[e_i, fb, k * 128:(k + 1) * 128, :, :],
                       inc="sw", inc_n=16)
                op("gpsimd", "dma_start",
                   out=WdnB[par][:].rearrange("p (f d) -> p f d", f=NFC),
                   in_=wdn4[e_i, :, fb * NFC:(fb + 1) * NFC, :],
                   inc="sw", inc_n=16)
                w_cnt = S.cnt["sw"]

                wait("tensor", "sw", w_cnt)
                for tt in range(NTT):
                    for fc in range(NFC):
                        gbank = psA if fc % 2 == 0 else psC
                        ubank = psB if fc % 2 == 0 else psD
                        for k in range(NK):
                            op("tensor", "matmul", gbank[:, 0:TT_],
                               lhsT=WguB[par][:, k * 1024 + fc * 128:
                                              k * 1024 + (fc + 1) * 128],
                               rhs=xnT_own[:, k * (n_own * CH) + tt * TT_:
                                           k * (n_own * CH) + (tt + 1) * TT_],
                               start=(k == 0), stop=(k == NK - 1),
                               inc=("sp" if k == NK - 1 else None))
                        for k in range(NK):
                            op("tensor", "matmul", ubank[:, 0:TT_],
                               lhsT=WguB[par][:, k * 1024 + 512 + fc * 128:
                                              k * 1024 + 512 + (fc + 1) * 128],
                               rhs=xnT_own[:, k * (n_own * CH) + tt * TT_:
                                           k * (n_own * CH) + (tt + 1) * TT_],
                               start=(k == 0), stop=(k == NK - 1),
                               inc=("sp" if k == NK - 1 else None))
                        sg = sgb[fc % 2]
                        aop("activation", sg[:, 0:TT_], gbank[:, 0:TT_], AF.Silu)
                        vop("tensor_mul", actb[fc][:, 0:TT_],
                            sg[:, 0:TT_], ubank[:, 0:TT_])
                    for gi in range(2 * ntok):
                        tok, dh = gi // 2, gi % 2
                        dbank = psW if gi % 2 == 0 else psV
                        for fc in range(NFC):
                            op("tensor", "matmul", dbank[:],
                               lhsT=actb[fc][:, tok * 128:(tok + 1) * 128],
                               rhs=WdnB[par][:, fc * 1024 + dh * 512:
                                             fc * 1024 + (dh + 1) * 512],
                               start=(fc == 0), stop=(fc == NFC - 1),
                               inc=("sp" if fc == NFC - 1 else None))
                        tc2 = tt * ntok + tok
                        vop("scalar_tensor_tensor",
                            acc[:, tc2 * D + dh * 512:tc2 * D + (dh + 1) * 512],
                            dbank[:], gates[:, tc2 * NE + e_i:tc2 * NE + e_i + 1],
                            acc[:, tc2 * D + dh * 512:tc2 * D + (dh + 1) * 512],
                            ALU.mult, ALU.add)
                wg_free[par] = S.cnt["sp"]
                it += 1

        if SPARSE and "B" in PHASES:
            NTT2 = CAP // 256
            sc_total = S.cnt.get("sg", 0)
            wait("gpsimd", "sg", sc_total)
            it = 0
            for e_i in range(NE):
                ep = e_i % 2
                xsem = "sxl" if ep == 0 else "sxm"
                if e_i == 0:
                    op("gpsimd", "dma_start",
                       out=xgl[0][:].rearrange("p (b d) -> p b d", b=NBL),
                       in_=xg[0:CAP, :].rearrange("(b p) d -> p b d", p=128),
                       inc="sxl", inc_n=16)
                if e_i + 1 < NE:
                    op("gpsimd", "dma_start",
                       out=xgl[1 - ep][:].rearrange("p (b d) -> p b d", b=NBL),
                       in_=xg[(e_i + 1) * CAP:(e_i + 2) * CAP, :].rearrange(
                           "(b p) d -> p b d", p=128),
                       inc=("sxm" if ep == 0 else "sxl"), inc_n=16)
                wait("tensor", xsem)
                for b in range(NBL):
                    for k in range(NK):
                        op("tensor", "transpose", psT[:, k * 128:(k + 1) * 128],
                           in_=xgl[ep][:, b * D + k * 128:b * D + (k + 1) * 128],
                           identity=identB[:], inc=("sp" if k == NK - 1 else None))
                    vop("tensor_copy",
                        xgT[:].rearrange("p (k t) -> p k t", k=NK)[
                            :, :, b * 128:(b + 1) * 128],
                        psT[:].rearrange("p (k t) -> p k t", k=NK))

                for fb in range(NFB):
                    par = it % 2
                    wait("gpsimd", "sp", wg_free[par])
                    wait("gpsimd", "sw")
                    for k in range(NK):
                        op("gpsimd", "dma_start",
                           out=WguB[par][:, k * 1024:(k + 1) * 1024].rearrange(
                               "p (h c) -> p h c", h=2),
                           in_=wgu5[e_i, fb, k * 128:(k + 1) * 128, :, :],
                           inc="sw", inc_n=16)
                    op("gpsimd", "dma_start",
                       out=WdnB[par][:].rearrange("p (f d) -> p f d", f=NFC),
                       in_=wdn4[e_i, :, fb * NFC:(fb + 1) * NFC, :],
                       inc="sw", inc_n=16)
                    w_cnt = S.cnt["sw"]
                    wait("tensor", "sw", w_cnt)

                    for tt in range(NTT2):
                        for fc in range(NFC):
                            bank = psA if fc % 2 == 0 else psB
                            for k in range(NK):
                                op("tensor", "matmul", bank[:, 0:256],
                                   lhsT=WguB[par][:, k * 1024 + fc * 128:
                                                  k * 1024 + (fc + 1) * 128],
                                   rhs=xgT[:, k * CAP + tt * 256:
                                           k * CAP + (tt + 1) * 256],
                                   start=(k == 0), stop=(k == NK - 1),
                                   inc=("sp" if k == NK - 1 else None))
                            for k in range(NK):
                                op("tensor", "matmul", bank[:, 256:512],
                                   lhsT=WguB[par][:, k * 1024 + 512 + fc * 128:
                                                  k * 1024 + 512 + (fc + 1) * 128],
                                   rhs=xgT[:, k * CAP + tt * 256:
                                           k * CAP + (tt + 1) * 256],
                                   start=(k == 0), stop=(k == NK - 1),
                                   inc=("sp" if k == NK - 1 else None))
                            sg_ = sgb[fc % 2]
                            aop("activation", sg_[:, 0:256], bank[:, 0:256], AF.Silu)
                            vop("tensor_mul", actb[fc][:, 0:256],
                                sg_[:, 0:256], bank[:, 256:512])
                        dbanks = (psC, psD, psW, psV)
                        for gi in range(4):
                            tok, dh = gi // 2, gi % 2
                            dbank = dbanks[gi]
                            ysl = yacc[gi][:, tt * 512:(tt + 1) * 512]
                            for fc in range(NFC):
                                last = (fc == NFC - 1) and fb == 0
                                op("tensor", "matmul", dbank[:],
                                   lhsT=actb[fc][:, tok * 128:(tok + 1) * 128],
                                   rhs=WdnB[par][:, fc * 1024 + dh * 512:
                                                 fc * 1024 + (dh + 1) * 512],
                                   start=(fc == 0), stop=last,
                                   inc=("sp" if last else None))
                            if fb > 0:
                                op("tensor", "matmul", dbank[:], lhsT=identB[:],
                                   rhs=ysl, start=False, stop=True, inc="sp")
                            if gi % 2 == 0:
                                vop("tensor_copy", ysl, dbank[:])
                            else:
                                aop("copy", ysl, dbank[:])
                    wg_free[par] = S.cnt["sp"]
                    it += 1
                for gi2 in range(4):
                    tok2, dh2 = gi2 // 2, gi2 % 2
                    op("gpsimd", "dma_start",
                       out=yout[e_i * CAP:(e_i + 1) * CAP, :].rearrange(
                           "(t b p) d -> p t b d", p=128, b=2)[
                           :, :, tok2, dh2 * 512:(dh2 + 1) * 512],
                       in_=yacc[gi2][:].rearrange("p (t d) -> p t d", d=512),
                       inc="sy", inc_n=16)
                for gi2 in range(4):
                    S._mark(S.readers, yacc[gi2].name, "sy", S.cnt["sy"])
            st_total = S.cnt["sy"]

            wait("gpsimd", "sy", st_total)
            gcnt = {0: st_total, 1: S.cnt.get("sg", 0)}
            for tc in range(n_own):
                for r in range(2):
                    gsem = "sy" if r == 0 else "sg"
                    op("gpsimd", "indirect_dma_start", out=ybk[r],
                       out_offset=None, in_=yout,
                       in_offset=IndirectOffsetOnAxis(
                           idxs[:, 2 * tc + r:2 * tc + r + 1], 0),
                       inc=gsem, inc_n=16)
                    wait("vector", gsem)
                    vop("scalar_tensor_tensor", acc[:, tc * D:(tc + 1) * D],
                        ybk[r], groutes[:, 2 * tc + r:2 * tc + r + 1],
                        acc[:, tc * D:(tc + 1) * D], ALU.mult, ALU.add)

        wait("sync", "sv")
        op("sync", "dma_start", out=out.rearrange("(i p) d -> p i d", p=128),
           in_=acc[:].rearrange("p (i d) -> p i d", d=D), inc="so", inc_n=16)
        wait("sync", "so")
        wait("gpsimd", "so")

        @block.sync
        def _(e):
            for fn in S.q["sync"]:
                fn(e)

        @block.gpsimd
        def _(e):
            for fn in S.q["gpsimd"]:
                fn(e)

        @block.tensor
        def _(e):
            for fn in S.q["tensor"]:
                fn(e)

        @block.vector
        def _(e):
            for fn in S.q["vector"]:
                fn(e)

        @block.scalar
        def _(e):
            for fn in S.q["scalar"]:
                fn(e)

    return nc


_NC_CACHE = None


def kernel(L0_attn_x, L0_attn_Wq, L0_attn_Wk, L0_attn_Wv, L0_attn_Wbeta,
           L0_attn_b_beta, L0_attn_Wout, L0_ffn_w_gate, L0_ffn_w_gate_up,
           L0_ffn_w_down):
    global _NC_CACHE
    x = np.ascontiguousarray(np.asarray(L0_attn_x, dtype=np.float32))
    common = dict(
        wq=np.ascontiguousarray(L0_attn_Wq, dtype=np.float32),
        wk=np.ascontiguousarray(L0_attn_Wk, dtype=np.float32),
        wv=np.ascontiguousarray(L0_attn_Wv, dtype=np.float32),
        wout=np.ascontiguousarray(L0_attn_Wout, dtype=np.float32),
        wbeta=np.ascontiguousarray(L0_attn_Wbeta, dtype=np.float32),
        bbeta=np.ascontiguousarray(L0_attn_b_beta, dtype=np.float32),
        wgate=np.ascontiguousarray(L0_ffn_w_gate, dtype=np.float32),
        wgu=np.ascontiguousarray(L0_ffn_w_gate_up, dtype=np.float32),
        wdn=np.ascontiguousarray(L0_ffn_w_down, dtype=np.float32),
        cident=np.eye(128, dtype=np.float32),
        cexp=np.tile((np.arange(NE) * CAP).astype(np.float32), (128, 1)),
        cslmask=np.tril(np.ones((128, 128), np.float32), -1),
        csumask=np.triu(np.ones((128, 128), np.float32), 0),
    )
    in_maps = []
    for c in range(8):
        b, h = c >> 1, c & 1
        if h == 0:
            xs = np.concatenate([np.zeros((2048, D), np.float32), x[b, 0:2048]], 0)
        else:
            xs = x[b]
        m = dict(common)
        m["xpad"] = np.ascontiguousarray(xs)
        in_maps.append(m)

    if _NC_CACHE is None:
        _NC_CACHE = build_program()
    res = run_bass_kernel_spmd(_NC_CACHE, in_maps, core_ids=list(range(8)))
    outp = np.empty((4, 4096, D), dtype=np.float32)
    for c in range(8):
        b, h = c >> 1, c & 1
        outp[b, h * 2048:(h + 1) * 2048, :] = res.results[c]["out"]
    return outp


# version 09
# speedup vs baseline: 1.1419x; 1.1419x over previous
import sys
import numpy as np
from contextlib import ExitStack

sys.path.insert(0, "/opt/trn_rl_repo")
import concourse.bass as bass
import concourse.mybir as mybir
from concourse.bass_utils import run_bass_kernel_spmd
from concourse.bass import IndirectOffsetOnAxis

F32 = mybir.dt.float32
BF16 = mybir.dt.bfloat16
AF = mybir.ActivationFunctionType
ALU = mybir.AluOpType
AX = mybir.AxisListType

D = 1024
NH = 16
DH = 64
NE = 8
FF = 4096
NCH = 32
OWN0 = 16
CH = 128
NK = D // 128
NFB = 8
NFC = 4
TT = 512
CAP = 768
SPARSE = True
PHASES = "AB"

ENGINES = ("sync", "gpsimd", "tensor", "vector", "scalar")
_ENGSEM = {"vector": "sv", "scalar": "sa"}
_PSUM = {"psA", "psB", "psC", "psD", "psW", "psV", "psT", "psU"}


class Sched:

    def __init__(self, sems):
        self.q = {e: [] for e in ENGINES}
        self.cnt = {}
        self.waited = {e: {} for e in ENGINES}
        self.sems = sems
        self.dirty = {}
        self.readers = {}
        self.psum_ready = {}
        self.pe_pending_w = set()
        self.pe_pending_r = set()

    def _mark(self, table, nm, sem, c):
        d = table.setdefault(nm, {})
        if d.get(sem, -1) < c:
            d[sem] = c

    def op(self, eng, method, *args, inc=None, inc_n=1, **kw):
        aps = [a for a in args if hasattr(a, "tensor")]
        for v in kw.values():
            if hasattr(v, "tensor"):
                aps.append(v)
            elif hasattr(v, "ap") and hasattr(v.ap, "tensor"):
                aps.append(v.ap)
        outa = kw.get("out")
        if outa is None and args and hasattr(args[0], "tensor"):
            outa = args[0]
        names = [(a.tensor.name, a is outa) for a in aps]
        for nm, _ in names:
            for sem, c in self.dirty.get(nm, {}).items():
                self.wait(eng, sem, c)
            if nm in _PSUM and eng != "tensor":
                pr = self.psum_ready.get(nm)
                if pr is not None:
                    self.wait(eng, "sp", pr)
        for nm, is_out in names:
            if is_out:
                assert nm not in self.pe_pending_r or eng == "tensor", (
                    f"write to {nm} while a PE read is not yet fenced")
                for sem, c in self.readers.get(nm, {}).items():
                    self.wait(eng, sem, c)
        if inc is not None:
            self.cnt[inc] = self.cnt.get(inc, 0) + inc_n
            sem, n = self.sems[inc], inc_n

            def fn(e, _m=method, _a=args, _k=kw, _s=sem, _n=n):
                getattr(e, _m)(*_a, **_k).then_inc(_s, _n)
        else:
            def fn(e, _m=method, _a=args, _k=kw):
                getattr(e, _m)(*_a, **_k)
        self.q[eng].append(fn)
        if eng == "tensor":
            for nm, is_out in names:
                if is_out:
                    self.pe_pending_w.add(nm)
                else:
                    self.pe_pending_r.add(nm)
            if inc is not None:
                c = self.cnt[inc]
                for nm in self.pe_pending_w:
                    self.psum_ready[nm] = c
                self.pe_pending_w.clear()
                for nm in self.pe_pending_r:
                    self._mark(self.readers, nm, "sp", c)
                self.pe_pending_r.clear()
        elif eng in _ENGSEM:
            assert inc == _ENGSEM[eng], f"writer on {eng} must inc: {method}"
            c = self.cnt[inc]
            for nm, is_out in names:
                if is_out:
                    self._mark(self.dirty, nm, _ENGSEM[eng], c)
                else:
                    self._mark(self.readers, nm, _ENGSEM[eng], c)
        elif inc is not None:
            c = self.cnt[inc]
            for nm, is_out in names:
                if not is_out:
                    self._mark(self.readers, nm, inc, c)
        return self.cnt.get(inc, 0) if inc else None

    def wait(self, eng, sem, val=None):
        if val is None:
            val = self.cnt.get(sem, 0)
        if val <= 0 or self.waited[eng].get(sem, -1) >= val:
            return
        self.waited[eng][sem] = val
        h = self.sems[sem]
        self.q[eng].append(lambda e, _h=h, _v=val: e.wait_ge(_h, _v))


def build_program():
    nc = bass.Bass("TRN2", target_bir_lowering=False)
    n_own = NCH - OWN0
    TT_ = min(TT, n_own * CH)
    NTT = (n_own * CH) // TT_
    ntok = TT_ // 128

    xpad = nc.dram_tensor("xpad", [NCH * CH, D], F32, kind="ExternalInput").ap()
    wq = nc.dram_tensor("wq", [D, D], F32, kind="ExternalInput").ap()
    wk = nc.dram_tensor("wk", [D, D], F32, kind="ExternalInput").ap()
    wv = nc.dram_tensor("wv", [D, D], F32, kind="ExternalInput").ap()
    wout = nc.dram_tensor("wout", [D, D], F32, kind="ExternalInput").ap()
    wbeta = nc.dram_tensor("wbeta", [D, NH], F32, kind="ExternalInput").ap()
    bbeta = nc.dram_tensor("bbeta", [NH], F32, kind="ExternalInput").ap()
    wgate = nc.dram_tensor("wgate", [D, NE], F32, kind="ExternalInput").ap()
    wgu = nc.dram_tensor("wgu", [NE, D, 2 * FF], F32, kind="ExternalInput").ap()
    wdn = nc.dram_tensor("wdn", [NE, FF, D], F32, kind="ExternalInput").ap()
    cident = nc.dram_tensor("cident", [128, 128], F32, kind="ExternalInput").ap()
    cslmask = nc.dram_tensor("cslmask", [128, 128], F32, kind="ExternalInput").ap()
    csumask = nc.dram_tensor("csumask", [128, 128], F32, kind="ExternalInput").ap()
    cexp = nc.dram_tensor("cexp", [128, NE], F32, kind="ExternalInput").ap()
    xg = nc.dram_tensor("xg", [NE * CAP, D], BF16, kind="Internal").ap()
    yout = nc.dram_tensor("yout", [NE * CAP, D], BF16, kind="Internal").ap()
    out = nc.dram_tensor("out", [n_own * CH, D], F32, kind="ExternalOutput").ap()

    with ExitStack() as ctx:
        sb = lambda n, s, d: ctx.enter_context(nc.sbuf_tensor(n, s, d))
        psd = lambda n, s, d: ctx.enter_context(nc.psum_tensor(n, s, d))

        SHARED = 106816
        shared = sb("shared", [128, SHARED], mybir.dt.uint8)
        shared_base = nc.lookup_mloc(shared).addr
        _roff = [0]

        def _ralloc(name, shape, dtype):
            per = 1
            for d_ in shape[1:]:
                per *= d_
            per *= mybir.dt.size(dtype)
            h = nc.alloc_sbuf_tensor_at(name, shape, dtype,
                                        offset=shared_base + _roff[0])
            _roff[0] = (_roff[0] + per + 31) // 32 * 32
            assert _roff[0] <= SHARED, f"shared region overflow at {name}"
            return h

        xnT_own = sb("xnT_own", [128, NK * n_own * CH], BF16)
        acc = sb("acc", [128, n_own * D], F32)
        gates = None if SPARSE else sb("gates", [128, n_own * NE], F32)
        M32 = sb("M32", [64, NH * DH], F32)
        M16 = sb("M16", [64, NH * DH], BF16)
        idxs = sb("idxs", [128, 2 * (NCH - OWN0)], mybir.dt.uint32)
        groutes = sb("groutes", [128, 2 * (NCH - OWN0)], F32)
        identB = sb("identB", [128, 128], BF16)
        identF = sb("identF", [16, 16], F32)
        slmask = sb("slmask", [128, 128], BF16)
        sumask = sb("sumask", [128, 128], BF16)

        WqB = _ralloc("WqB", [128, NK * D], BF16)
        WkB = _ralloc("WkB", [128, NK * D], BF16)
        WvB = _ralloc("WvB", [128, NK * D], BF16)
        WoB = _ralloc("WoB", [128, NK * D], BF16)
        WbB = _ralloc("WbB", [128, NK * NH], BF16)
        WgB = _ralloc("WgB", [128, NK * NE], BF16)
        bbn = _ralloc("bbn", [16, 1], F32)
        bbp = _ralloc("bbp", [16, 1], F32)
        xin = _ralloc("xin0", [128, D], F32)
        xnb = _ralloc("xnb", [128, D], BF16)
        KT = _ralloc("KTb", [128, 8 * 128], BF16)
        VT = _ralloc("VTb", [128, 8 * 128], BF16)
        QT = _ralloc("QTb", [128, 8 * 128], BF16)
        betaT = _ralloc("betaT", [16, 128], F32)
        bcols = _ralloc("bcols", [128, 16], F32)
        bcolsn = _ralloc("bcolsn", [128, 16], F32)
        zgs = _ralloc("zgs", [8, 128], F32)
        zt = _ralloc("ztb", [128, 8], F32)
        Ktok = [_ralloc(f"Ktok{g}", [128, 8 * DH], BF16) for g in range(2)]
        Kn = Ktok
        Vtb = [_ralloc(f"Vtb{g}", [128, 8 * DH], BF16) for g in range(2)]
        Rt = Vtb
        KnT = [_ralloc(f"KnT{g}", [64, 8 * 128], BF16) for g in range(2)]
        ssb = [_ralloc(f"ssb{g}", [128, 8], F32) for g in range(2)]
        invb = [_ralloc(f"invb{g}", [128, 8], F32) for g in range(2)]
        Gb = [_ralloc(f"Gb{g}", [128, 8 * 128], BF16) for g in range(2)]
        Qh0 = Gb
        Gp = [_ralloc(f"Gp{g}", [128, 8 * 128], BF16) for g in range(2)]
        Pk = Gb
        Pq = Gp
        Sa = [_ralloc(f"Sa{g}", [128, 8 * 128], BF16) for g in range(2)]
        Xg = [_ralloc(f"Xg{g}", [128, 8 * DH], BF16) for g in range(2)]
        Lg = Sa
        Oall = [_ralloc(f"Oall{g}", [128, 8 * DH], BF16) for g in range(2)]
        Qtok = Oall
        OTb = _ralloc("OTb", [128, 8 * 128], BF16)
        c_s1 = _ralloc("c_s1", [128, 1], F32)
        c_mean = _ralloc("c_mean", [128, 1], F32)
        c_s2 = _ralloc("c_s2", [128, 1], F32)
        c_m2 = _ralloc("c_m2", [128, 1], F32)
        c_var = _ralloc("c_var", [128, 1], F32)
        c_rstd = _ralloc("c_rstd", [128, 1], F32)
        g_nm = _ralloc("g_nm", [128, 1], F32)
        g_ex = _ralloc("g_ex", [128, 8], F32)
        g_sum = _ralloc("g_sum", [128, 1], F32)
        g_rs = _ralloc("g_rs", [128, 1], F32)
        g_p = _ralloc("g_p", [128, 8], F32)
        g_m1 = _ralloc("g_m1", [128, 1], F32)
        g_m2 = _ralloc("g_m2", [128, 1], F32)
        g_mv = _ralloc("g_mv", [128, 8], F32)
        g_pm1 = _ralloc("g_pm1", [128, 8], F32)
        g_pm2 = _ralloc("g_pm2", [128, 8], F32)
        g_ne = _ralloc("g_ne", [128, 8], F32)
        g_den = _ralloc("g_den", [128, 1], F32)
        r_cnt8 = sb("r_cnt8", [8, 1], F32)
        r_tmp8 = sb("r_tmp8", [8, 1], F32)
        r_snzb = sb("r_snzb", [128, 8], BF16)
        r_m1 = sb("r_m1", [128, 8], F32)
        r_m2 = sb("r_m2", [128, 8], F32)
        r_t = sb("r_t", [128, 8], F32)
        r_o1 = sb("r_o1", [128, 1], F32)
        r_o2 = sb("r_o2", [128, 1], F32)
        capexp = sb("capexp", [128, NE], F32)
        A_BYTES = _roff[0]
        import os
        if os.environ.get("PRINT_AB"): print("A_BYTES", A_BYTES)

        _roff[0] = 0
        WguB = [_ralloc("WguB0", [128, NK * 1024], BF16),
                _ralloc("WguB1", [128, NK * 1024], BF16)]
        WdnB = [_ralloc("WdnB0", [128, NFC * 1024], BF16),
                _ralloc("WdnB1", [128, NFC * 1024], BF16)]
        actb = [_ralloc(f"actb{i}", [128, 512], BF16) for i in range(NFC)]
        sgb = [_ralloc("sgb0", [128, 256], F32), _ralloc("sgb1", [128, 256], F32)]
        NBL = CAP // 128
        xgl = [_ralloc("xgl0", [128, NBL * D], BF16),
               _ralloc("xgl1", [128, NBL * D], BF16)]
        xgT = _ralloc("xgT", [128, NK * CAP], BF16)
        yacc = [_ralloc(f"yacc{i}", [128, (NBL // 2) * 512], BF16) for i in range(4)]
        ybk = [xgl[0][:, 0:D], xgl[0][:, D:2 * D]]
        if os.environ.get("PRINT_AB"): print("B_BYTES", _roff[0])

        psA = psd("psA", [128, 512], F32)
        psB = psd("psB", [128, 512], F32)
        psC = psd("psC", [128, 512], F32)
        psD = psd("psD", [128, 512], F32)
        psW = psd("psW", [128, 512], F32)
        psV = psd("psV", [128, 512], F32)
        psT = psd("psT", [128, 1024], BF16)
        psU = psd("psU", [128, 1024], BF16)
        fpair = [(psA, psB), (psC, psD)]
        trb = [psT, psU]
        kb = [psA, psC]

        sems = {}
        for name in ("sc", "si", "sxe", "sxo", "swp", "sw", "so",
                     "sg", "sy", "sxl", "sxm", "sp", "sv", "sa"):
            sems[name] = ctx.enter_context(nc.semaphore(name))
        S = Sched(sems)
        block = ctx.enter_context(nc.Block())
        op, wait = S.op, S.wait

        def vop(method, *args, **kw):
            op("vector", method, *args, inc="sv", **kw)

        def aop(method, *args, **kw):
            op("scalar", method, *args, inc="sa", **kw)

        def F32T(g, hl):
            bank = fpair[g][0] if hl < 4 else fpair[g][1]
            s_ = hl % 4
            return bank[:, s_ * 128:(s_ + 1) * 128]

        op("sync", "dma_start", out=identF[:], in_=cident[0:16, 0:16], inc="sc", inc_n=16)
        op("gpsimd", "dma_start", out=slmask[:], in_=cslmask, inc="swp", inc_n=16)
        op("gpsimd", "dma_start", out=sumask[:], in_=csumask, inc="swp", inc_n=16)
        op("sync", "dma_start", out=bbp[:],
           in_=bbeta.rearrange("(p o) -> p o", o=1), inc="sc", inc_n=16)
        op("sync", "dma_start",
           out=acc[:].rearrange("p (i d) -> p i d", d=D),
           in_=xpad[OWN0 * CH:NCH * CH, :].rearrange("(i p) d -> p i d", p=128),
           inc="si", inc_n=16)

        op("gpsimd", "dma_start", out=identB[:], in_=cident, inc="swp", inc_n=16)
        for wsrc, wdst in ((wq, WqB), (wk, WkB), (wv, WvB), (wout, WoB)):
            for k in range(NK):
                op("gpsimd", "dma_start", out=wdst[:, k * D:(k + 1) * D],
                   in_=wsrc[k * 128:(k + 1) * 128, :], inc="swp", inc_n=16)
        op("gpsimd", "dma_start",
           out=WbB[:].rearrange("p (k h) -> p k h", k=NK),
           in_=wbeta.rearrange("(k p) h -> p k h", p=128), inc="swp", inc_n=16)
        op("gpsimd", "dma_start",
           out=WgB[:].rearrange("p (k h) -> p k h", k=NK),
           in_=wgate.rearrange("(k p) h -> p k h", p=128), inc="swp", inc_n=16)
        proj_w = S.cnt["swp"]

        op("sync", "dma_start", out=capexp[:], in_=cexp, inc="sc", inc_n=16)
        wait("vector", "sc")
        if SPARSE:
            vop("memset", xnb[:], 0.0)
            wait("gpsimd", "sv")
            for b in range(NE * CAP // 128):
                op("gpsimd", "dma_start",
                   out=xg[b * 128:(b + 1) * 128, :], in_=xnb[:],
                   inc="sg", inc_n=16)
        vop("memset", r_cnt8[:], 0.0)
        vop("tensor_scalar_mul", bbn[:], bbp[:], -1.0)
        vop("memset", M32[:], 0.0)
        vop("memset", M16[:], 0.0)

        wait("tensor", "sc")
        wait("tensor", "swp", proj_w)
        def emit_ln(c):
            sxp = "sxe" if c % 2 == 0 else "sxo"
            op("sync", "dma_start", out=xin[:], in_=xpad[c * CH:(c + 1) * CH, :],
               inc=sxp, inc_n=16)
            wait("vector", sxp)
            wait("vector", "sg")
            vop("reduce_sum", c_s1[:], xin[:], axis=AX.X)
            vop("tensor_scalar_mul", c_mean[:], c_s1[:], 1.0 / D)
            vop("tensor_mul", xnb[:], xin[:], xin[:])
            vop("reduce_sum", c_s2[:], xnb[:], axis=AX.X)
            vop("tensor_mul", c_m2[:], c_mean[:], c_mean[:])
            vop("scalar_tensor_tensor", c_var[:], c_s2[:], 1.0 / D, c_m2[:],
                ALU.mult, ALU.subtract)
            vop("tensor_scalar_add", c_var[:], c_var[:], 1e-5)
            aop("activation", c_rstd[:], c_var[:], AF.Ln)
            aop("activation", c_rstd[:], c_rstd[:], AF.Exp, scale=-0.5)
            vop("tensor_scalar", xnb[:], xin[:], c_mean[:], c_rstd[:],
                ALU.subtract, ALU.mult)

        def front_pieces(c):
            own = c >= OWN0
            tc = c - OWN0
            slot = tc if own else (c & 1)

            def XN(k, _slot=slot):
                base = k * (n_own * CH) + _slot * CH
                return xnT_own[:, base:base + CH]

            def p_xnt():
                for i in range(NK):
                    op("tensor", "transpose", psT[:, i * 128:(i + 1) * 128],
                       in_=xnb[:, i * 128:(i + 1) * 128], identity=identB[:],
                       inc=("sp" if i == NK - 1 else None))
                xd = xnT_own[:].rearrange("p (k t) -> p k t", k=NK)[
                    :, :, slot * CH:(slot + 1) * CH]
                xs = psT[:].rearrange("p (k t) -> p k t", k=NK)
                vop("tensor_copy", xd[:, 0:NK // 2, :], xs[:, 0:NK // 2, :])
                aop("copy", xd[:, NK // 2:NK, :], xs[:, NK // 2:NK, :])

            def mk_proj(wbuf, dst, half, bank):
                def p():
                    for i in range(4):
                        ii = half * 4 + i
                        for k in range(NK):
                            op("tensor", "matmul", bank[:, i * 128:(i + 1) * 128],
                               lhsT=wbuf[:, k * D + ii * 128:k * D + (ii + 1) * 128],
                               rhs=XN(k), start=(k == 0), stop=(k == NK - 1),
                               inc=("sp" if (i == 3 and k == NK - 1) else None))
                    vop("tensor_copy", dst[:, half * 512:half * 512 + 256],
                        bank[:, 0:256])
                    aop("copy", dst[:, half * 512 + 256:(half + 1) * 512],
                        bank[:, 256:512])
                return p

            def p_beta_mm():
                for k in range(NK):
                    op("tensor", "matmul", psV[0:16, 0:128],
                       lhsT=WbB[:, k * NH:(k + 1) * NH], rhs=XN(k),
                       start=(k == 0), stop=(k == NK - 1),
                       inc=("sp" if k == NK - 1 else None))
                aop("activation", betaT[:], psV[0:16, 0:128], AF.Exp,
                    scale=-1.0, bias=bbn[:])
                vop("tensor_scalar_add", betaT[:], betaT[:], 1.0)
                vop("reciprocal", betaT[:], betaT[:])

            def p_beta_fin():
                op("tensor", "transpose", psV[:, 288:304], in_=betaT[:],
                   identity=identF[:], inc="sp")
                vop("tensor_copy", bcols[:], psV[:, 288:304])
                vop("tensor_scalar_mul", bcolsn[:], bcols[:], -1.0)

            def p_gate_mm():
                for k in range(NK):
                    op("tensor", "matmul", psV[0:8, 128:256],
                       lhsT=WgB[:, k * NE:(k + 1) * NE], rhs=XN(k),
                       start=(k == 0), stop=(k == NK - 1),
                       inc=("sp" if k == NK - 1 else None))
                vop("tensor_copy", zgs[:], psV[0:8, 128:256])

            def p_gate_fin():
                op("tensor", "transpose", psV[:, 304:312], in_=zgs[:],
                   identity=identF[0:8, 0:8], inc="sp")
                vop("tensor_copy", zt[:], psV[:, 304:312])
                vop("reduce_max", g_nm[:], zt[:], axis=AX.X, negate=True)
                aop("activation", g_ex[:], zt[:], AF.Exp, bias=g_nm[:])
                vop("reduce_sum", g_sum[:], g_ex[:], axis=AX.X)
                vop("reciprocal", g_rs[:], g_sum[:])
                vop("tensor_scalar_mul", g_p[:], g_ex[:], g_rs[:])
                vop("reduce_max", g_m1[:], g_p[:], axis=AX.X)
                vop("memset", g_mv[:], -1e30)
                vop("tensor_copy", g_mv[:, 0:1], g_m1[:])
                vop("match_replace", g_pm1[:], in_to_replace=g_mv[:],
                    in_values=g_p[:], imm_value=-1.0)
                vop("reduce_max", g_m2[:], g_pm1[:], axis=AX.X)
                vop("tensor_copy", g_mv[:, 0:1], g_m2[:])
                vop("match_replace", g_pm2[:], in_to_replace=g_mv[:],
                    in_values=g_pm1[:], imm_value=-1.0)
                vop("tensor_tensor", g_ne[:], g_p[:], g_pm2[:], ALU.not_equal)
                vop("tensor_add", g_den[:], g_m1[:], g_m2[:])
                vop("tensor_scalar_max", g_den[:], g_den[:], 1e-9)
                vop("reciprocal", g_den[:], g_den[:])
                if not SPARSE:
                    vop("tensor_mul", g_ne[:], g_ne[:], g_p[:])
                    vop("tensor_scalar", gates[:, tc * NE:(tc + 1) * NE],
                        g_ne[:], g_den[:], None, ALU.mult)

            def p_route():
                vop("tensor_tensor", r_m1[:], g_p[:], g_pm1[:], ALU.not_equal)
                vop("tensor_tensor", r_m2[:], g_pm1[:], g_pm2[:], ALU.not_equal)
                vop("tensor_add", r_t[:], r_m1[:], r_m2[:])
                vop("tensor_copy", r_snzb[:], r_t[:])

            def p_route_b():
                op("tensor", "transpose", psT[0:8, 0:128], in_=r_snzb[:],
                   identity=identB[:], inc="sp")
                vop("tensor_copy", zgs[:], psT[0:8, 0:128])
                vop("tensor_tensor_scan", betaT[0:8, :], zgs[:],
                    zgs[:], 0.0, ALU.add, ALU.max)
                vop("tensor_scalar", betaT[0:8, :], betaT[0:8, :], r_cnt8[:], -1.0,
                    ALU.add, ALU.add)
                vop("reduce_sum", r_tmp8[:], zgs[:], axis=AX.X)
                vop("tensor_add", r_cnt8[:], r_cnt8[:], r_tmp8[:])

            def p_route_c():
                op("tensor", "transpose", psV[:, 312:320], in_=betaT[0:8, :],
                   identity=identF[0:8, 0:8], inc="sp")
                vop("tensor_tensor", r_t[:], psV[:, 312:320], capexp[:], ALU.add)
                vop("tensor_mul", g_ex[:], r_t[:], r_m1[:])
                vop("reduce_sum", r_o1[:], g_ex[:], axis=AX.X)
                vop("tensor_mul", g_ex[:], r_t[:], r_m2[:])
                vop("reduce_sum", r_o2[:], g_ex[:], axis=AX.X)
                vop("tensor_scalar_min", r_o1[:], r_o1[:], float(NE * CAP - 1))
                vop("tensor_scalar_min", r_o2[:], r_o2[:], float(NE * CAP - 1))
                vop("tensor_copy", idxs[:, 2 * tc:2 * tc + 1], r_o1[:])
                vop("tensor_copy", idxs[:, 2 * tc + 1:2 * tc + 2], r_o2[:])
                vop("tensor_mul", groutes[:, 2 * tc:2 * tc + 1], g_m1[:], g_den[:])
                vop("tensor_mul", groutes[:, 2 * tc + 1:2 * tc + 2], g_m2[:], g_den[:])
                op("gpsimd", "indirect_dma_start", out=xg,
                   out_offset=IndirectOffsetOnAxis(idxs[:, 2 * tc:2 * tc + 1], 0),
                   in_=xnb[:], in_offset=None, inc="sg", inc_n=16)
                op("gpsimd", "indirect_dma_start", out=xg,
                   out_offset=IndirectOffsetOnAxis(idxs[:, 2 * tc + 1:2 * tc + 2], 0),
                   in_=xnb[:], in_offset=None, inc="sg", inc_n=16)

            pieces = [p_xnt,
                      mk_proj(WkB, KT, 0, psW), mk_proj(WkB, KT, 1, psV),
                      mk_proj(WvB, VT, 0, psW), mk_proj(WvB, VT, 1, psV)]
            if own:
                pieces += [mk_proj(WqB, QT, 0, psW), mk_proj(WqB, QT, 1, psV)]
            pieces += [p_beta_mm, p_beta_fin]
            if own:
                pieces += [p_gate_mm, p_gate_fin]
                if SPARSE:
                    pieces += [p_route, p_route_b, p_route_c]
            return pieces

        def chunk_steps(c):
            own = c >= OWN0
            tc = c - OWN0

            def PTile(buf, g, p):
                t = g * 4 + p
                return buf[:, t * 128:(t + 1) * 128]

            def BC(g, hl):
                j = g * 8 + hl
                return bcols[:, j:j + 1]

            def BCN(g, hl):
                j = g * 8 + hl
                return bcolsn[:, j:j + 1]

            def cop2(dst_t, dsl, src_t, ssl, rows=None):
                (dl, dh_), (sl2, sh2) = dsl, ssl
                mid = (dh_ - dl) // 2
                r = slice(0, 128) if rows is None else rows
                vop("tensor_copy", dst_t[r, dl:dl + mid], src_t[r, sl2:sl2 + mid])
                aop("copy", dst_t[r, dl + mid:dh_], src_t[r, sl2 + mid:sh2])

            def s_t1(g):
                tr = trb[g]
                for p in range(4):
                    op("tensor", "transpose", tr[:, p * 128:(p + 1) * 128],
                       in_=PTile(KT, g, p), identity=identB[:],
                       inc=("sp" if p == 3 else None))
                cop2(Ktok[g], (0, 512), tr, (0, 512))

            def s_t5a(g):
                tr = trb[g]
                for p in range(4):
                    op("tensor", "transpose", tr[:, p * 128:(p + 1) * 128],
                       in_=PTile(VT, g, p), identity=identB[:],
                       inc=("sp" if p == 3 else None))
                for hl in range(8):
                    pp, b = hl // 2, 64 * (hl % 2)
                    if g == 0:
                        vop("tensor_scalar_mul", Vtb[g][:, hl * DH:(hl + 1) * DH],
                            tr[:, pp * 128 + b:pp * 128 + b + 64], BC(g, hl))
                    else:
                        aop("activation", Vtb[g][:, hl * DH:(hl + 1) * DH],
                            tr[:, pp * 128 + b:pp * 128 + b + 64], AF.Copy,
                            scale=BC(g, hl))

            def s_q1(g):
                tr = trb[g]
                for p in range(4):
                    op("tensor", "transpose", tr[:, p * 128:(p + 1) * 128],
                       in_=PTile(QT, g, p), identity=identB[:],
                       inc=("sp" if p == 3 else None))
                cop2(Qtok[g], (0, 512), tr, (0, 512))

            def s_t2(g):
                vop("tensor_mul", kb[g][:], Ktok[g][:], Ktok[g][:])
                vop("reduce_sum", ssb[g][:],
                    kb[g][:].rearrange("p (h d) -> p h d", h=8), axis=AX.X)
                vop("tensor_scalar_max", ssb[g][:], ssb[g][:], 1e-24)
                aop("activation", invb[g][:], ssb[g][:], AF.Ln)
                aop("activation", invb[g][:], invb[g][:], AF.Exp, scale=-0.5)
                for hl in range(8):
                    vop("tensor_scalar_mul", Kn[g][:, hl * DH:(hl + 1) * DH],
                        Ktok[g][:, hl * DH:(hl + 1) * DH], invb[g][:, hl:hl + 1])

            def s_t4(g):
                tr = trb[g]
                for hl in range(8):
                    op("tensor", "transpose", tr[0:64, hl * 128:(hl + 1) * 128],
                       in_=Kn[g][:, hl * DH:(hl + 1) * DH], identity=identB[:],
                       inc=("sp" if hl == 7 else None))
                cop2(KnT[g], (0, 1024), tr, (0, 1024), rows=slice(0, 64))

            def s_t5b(g):
                for hl in range(8):
                    hg = g * 8 + hl
                    op("tensor", "matmul", kb[g][:, hl * DH:(hl + 1) * DH],
                       lhsT=KnT[g][0:64, hl * 128:(hl + 1) * 128],
                       rhs=M16[0:64, hg * DH:(hg + 1) * DH],
                       start=True, stop=True, inc=("sp" if hl == 7 else None))
                for hl in range(8):
                    vop("scalar_tensor_tensor", Rt[g][:, hl * DH:(hl + 1) * DH],
                        kb[g][:, hl * DH:(hl + 1) * DH], BCN(g, hl),
                        Vtb[g][:, hl * DH:(hl + 1) * DH], ALU.mult, ALU.add)

            def s_t6(g):
                for hl in range(8):
                    op("tensor", "matmul", F32T(g, hl),
                       lhsT=KnT[g][0:64, hl * 128:(hl + 1) * 128],
                       rhs=KnT[g][0:64, hl * 128:(hl + 1) * 128],
                       start=True, stop=True, inc=("sp" if hl == 7 else None))
                for hl in range(8):
                    vop("scalar_tensor_tensor", Gb[g][:, hl * 128:(hl + 1) * 128],
                        F32T(g, hl), BC(g, hl), slmask[:], ALU.mult, ALU.mult)

            def s_t7(g):
                tr = trb[g]
                for hl in range(8):
                    op("tensor", "transpose", tr[:, hl * 128:(hl + 1) * 128],
                       in_=Gb[g][:, hl * 128:(hl + 1) * 128], identity=identB[:],
                       inc=("sp" if hl == 7 else None))
                cop2(Gp[g], (0, 1024), tr, (0, 1024))
                for hl in range(8):
                    vop("tensor_sub", Sa[g][:, hl * 128:(hl + 1) * 128],
                        identB[:], Gp[g][:, hl * 128:(hl + 1) * 128])

            def s_lev_p(g, lev):
                for hl in range(8):
                    sl_ = slice(hl * 128, (hl + 1) * 128)
                    op("tensor", "matmul", F32T(g, hl),
                       lhsT=Pq[g][:, sl_], rhs=Pk[g][:, sl_],
                       start=True, stop=True, inc=("sp" if hl == 7 else None))
                vop("tensor_copy", Pk[g][:, 0:512], fpair[g][0][:])
                aop("copy", Pk[g][:, 512:1024], fpair[g][1][:])

            def s_lev_pt(g, lev):
                tr = trb[g]
                for hl in range(8):
                    op("tensor", "transpose", tr[:, hl * 128:(hl + 1) * 128],
                       in_=Pk[g][:, hl * 128:(hl + 1) * 128], identity=identB[:],
                       inc=("sp" if hl == 7 else None))
                cop2(Pq[g], (0, 1024), tr, (0, 1024))

            def s_lev_s(g, lev):
                for hl in range(8):
                    sl_ = slice(hl * 128, (hl + 1) * 128)
                    op("tensor", "matmul", F32T(g, hl), lhsT=Pk[g][:, sl_],
                       rhs=Sa[g][:, sl_], start=True, stop=False)
                    op("tensor", "matmul", F32T(g, hl), lhsT=identB[:],
                       rhs=Sa[g][:, sl_], start=False, stop=True,
                       inc=("sp" if hl == 7 else None))
                vop("tensor_copy", Sa[g][:, 0:512], fpair[g][0][:])
                aop("copy", Sa[g][:, 512:1024], fpair[g][1][:])

            def s_t9(g):
                for hl in range(8):
                    op("tensor", "matmul", kb[g][:, hl * DH:(hl + 1) * DH],
                       lhsT=Sa[g][:, hl * 128:(hl + 1) * 128],
                       rhs=Rt[g][:, hl * DH:(hl + 1) * DH],
                       start=True, stop=True, inc=("sp" if hl == 7 else None))
                cop2(Xg[g], (0, 512), kb[g], (0, 512))

            def s_t10(g):
                for hl in range(8):
                    op("tensor", "matmul", kb[g][0:64, hl * DH:(hl + 1) * DH],
                       lhsT=Kn[g][:, hl * DH:(hl + 1) * DH],
                       rhs=Xg[g][:, hl * DH:(hl + 1) * DH],
                       start=True, stop=True, inc=("sp" if hl == 7 else None))
                for hl in range(8):
                    hg = g * 8 + hl
                    vop("tensor_tensor", M32[:, hg * DH:(hg + 1) * DH],
                        M32[:, hg * DH:(hg + 1) * DH],
                        kb[g][0:64, hl * DH:(hl + 1) * DH], ALU.add)

            def s_q2(g):
                tr = trb[g]
                for hl in range(8):
                    op("tensor", "transpose", tr[0:64, hl * 128:(hl + 1) * 128],
                       in_=Qtok[g][:, hl * DH:(hl + 1) * DH], identity=identB[:],
                       inc=("sp" if hl == 7 else None))
                cop2(Qh0[g], (0, 1024), tr, (0, 1024), rows=slice(0, 64))

            def s_kq(g):
                for hl in range(8):
                    op("tensor", "matmul", F32T(g, hl),
                       lhsT=KnT[g][0:64, hl * 128:(hl + 1) * 128],
                       rhs=Qh0[g][0:64, hl * 128:(hl + 1) * 128],
                       start=True, stop=True, inc=("sp" if hl == 7 else None))
                for hl in range(8):
                    vop("tensor_mul", Lg[g][:, hl * 128:(hl + 1) * 128],
                        F32T(g, hl), sumask[:])

            def s_o(g):
                for hl in range(8):
                    hg = g * 8 + hl
                    op("tensor", "matmul", kb[g][:, hl * DH:(hl + 1) * DH],
                       lhsT=Qh0[g][0:64, hl * 128:(hl + 1) * 128],
                       rhs=M16[0:64, hg * DH:(hg + 1) * DH],
                       start=True, stop=False)
                    op("tensor", "matmul", kb[g][:, hl * DH:(hl + 1) * DH],
                       lhsT=Lg[g][:, hl * 128:(hl + 1) * 128],
                       rhs=Xg[g][:, hl * DH:(hl + 1) * DH],
                       start=False, stop=True, inc=("sp" if hl == 7 else None))
                cop2(Oall[g], (0, 512), kb[g], (0, 512))

            def s_ot(g):
                tr = trb[g]
                for p in range(4):
                    op("tensor", "transpose", tr[:, p * 128:(p + 1) * 128],
                       in_=Oall[g][:, p * 128:(p + 1) * 128], identity=identB[:],
                       inc=("sp" if p == 3 else None))
                cop2(OTb, (g * 512, (g + 1) * 512), tr, (0, 512))

            base = [s_t1, s_t5a]
            if own:
                base.append(s_q1)
            base += [s_t2, s_t4, s_t5b, s_t6, s_t7]
            for lev in range(1, 7):
                base.append(lambda g, _l=lev: s_lev_p(g, _l))
                if lev < 6:
                    base.append(lambda g, _l=lev: s_lev_pt(g, _l))
                base.append(lambda g, _l=lev: s_lev_s(g, _l))
            base += [s_t9, s_t10]
            if own:
                base += [s_q2, s_kq, s_o, s_ot]
            units = []
            for st in base:
                units.append(lambda _s=st: _s(0))
                units.append(lambda _s=st: _s(1))
                if (own and st is s_o) or (not own and st is s_t10):
                    units.append(lambda: cop2(M16, (0, 1024), M32, (0, 1024),
                                              rows=slice(0, 64)))

            if own:
                def p_wout():
                    for i in range(8):
                        bank = psC if i < 4 else psD
                        for k in range(NK):
                            op("tensor", "matmul",
                               bank[:, (i % 4) * 128:((i % 4) + 1) * 128],
                               lhsT=WoB[:, k * D + i * 128:k * D + (i + 1) * 128],
                               rhs=OTb[:, k * 128:(k + 1) * 128],
                               start=(k == 0), stop=(k == NK - 1),
                               inc=("sp" if (i == 7 and k == NK - 1) else None))
                    vop("tensor_copy", Lg[1][:, 0:512], psC[:])
                    aop("copy", Lg[1][:, 512:1024], psD[:])
                    for i in range(8):
                        op("tensor", "transpose", psT[:, i * 128:(i + 1) * 128],
                           in_=Lg[1][:, i * 128:(i + 1) * 128], identity=identB[:],
                           inc=("sp" if i == 7 else None))
                    wait("vector", "si", 16)
                    vop("tensor_tensor", acc[:, tc * D:(tc + 1) * D],
                        acc[:, tc * D:(tc + 1) * D], psT[:], ALU.add)
                units.append(p_wout)
            return units

        if "A" in PHASES:
            emit_ln(0)
            for p in front_pieces(0):
                p()
        for c in (range(NCH) if "A" in PHASES else []):
            units = chunk_steps(c)
            if c + 1 < NCH:
                nf = [lambda _c=c + 1: emit_ln(_c)] + front_pieces(c + 1)
            else:
                nf = []
            nu, nn = len(units), len(nf)
            k = 0
            for i, u in enumerate(units):
                u()
                while k < nn and (i + 1) * nn >= (k + 1) * nu:
                    nf[k]()
                    k += 1
            while k < nn:
                nf[k]()
                k += 1

        wait("tensor", "sv")
        wait("tensor", "sa")
        pa_sp = S.cnt.get("sp", 0)
        wait("gpsimd", "sp", pa_sp)
        wait("gpsimd", "sv")
        wait("gpsimd", "sa")
        wgu5 = wgu.rearrange("e r (h f c) -> e f r h c", h=2, f=NFB)
        wdn4 = wdn.rearrange("e (f p) d -> e p f d", p=128)

        wg_free = [pa_sp, pa_sp]
        it = 0
        for e_i in range(0 if (SPARSE or "B" not in PHASES) else NE):
            for fb in range(NFB):
                par = it % 2
                wait("gpsimd", "sp", wg_free[par])
                wait("gpsimd", "sw")
                for k in range(NK):
                    op("gpsimd", "dma_start",
                       out=WguB[par][:, k * 1024:(k + 1) * 1024].rearrange(
                           "p (h c) -> p h c", h=2),
                       in_=wgu5[e_i, fb, k * 128:(k + 1) * 128, :, :],
                       inc="sw", inc_n=16)
                op("gpsimd", "dma_start",
                   out=WdnB[par][:].rearrange("p (f d) -> p f d", f=NFC),
                   in_=wdn4[e_i, :, fb * NFC:(fb + 1) * NFC, :],
                   inc="sw", inc_n=16)
                w_cnt = S.cnt["sw"]

                wait("tensor", "sw", w_cnt)
                for tt in range(NTT):
                    for fc in range(NFC):
                        gbank = psA if fc % 2 == 0 else psC
                        ubank = psB if fc % 2 == 0 else psD
                        for k in range(NK):
                            op("tensor", "matmul", gbank[:, 0:TT_],
                               lhsT=WguB[par][:, k * 1024 + fc * 128:
                                              k * 1024 + (fc + 1) * 128],
                               rhs=xnT_own[:, k * (n_own * CH) + tt * TT_:
                                           k * (n_own * CH) + (tt + 1) * TT_],
                               start=(k == 0), stop=(k == NK - 1),
                               inc=("sp" if k == NK - 1 else None))
                        for k in range(NK):
                            op("tensor", "matmul", ubank[:, 0:TT_],
                               lhsT=WguB[par][:, k * 1024 + 512 + fc * 128:
                                              k * 1024 + 512 + (fc + 1) * 128],
                               rhs=xnT_own[:, k * (n_own * CH) + tt * TT_:
                                           k * (n_own * CH) + (tt + 1) * TT_],
                               start=(k == 0), stop=(k == NK - 1),
                               inc=("sp" if k == NK - 1 else None))
                        sg = sgb[fc % 2]
                        aop("activation", sg[:, 0:TT_], gbank[:, 0:TT_], AF.Silu)
                        vop("tensor_mul", actb[fc][:, 0:TT_],
                            sg[:, 0:TT_], ubank[:, 0:TT_])
                    for gi in range(2 * ntok):
                        tok, dh = gi // 2, gi % 2
                        dbank = psW if gi % 2 == 0 else psV
                        for fc in range(NFC):
                            op("tensor", "matmul", dbank[:],
                               lhsT=actb[fc][:, tok * 128:(tok + 1) * 128],
                               rhs=WdnB[par][:, fc * 1024 + dh * 512:
                                             fc * 1024 + (dh + 1) * 512],
                               start=(fc == 0), stop=(fc == NFC - 1),
                               inc=("sp" if fc == NFC - 1 else None))
                        tc2 = tt * ntok + tok
                        vop("scalar_tensor_tensor",
                            acc[:, tc2 * D + dh * 512:tc2 * D + (dh + 1) * 512],
                            dbank[:], gates[:, tc2 * NE + e_i:tc2 * NE + e_i + 1],
                            acc[:, tc2 * D + dh * 512:tc2 * D + (dh + 1) * 512],
                            ALU.mult, ALU.add)
                wg_free[par] = S.cnt["sp"]
                it += 1

        if SPARSE and "B" in PHASES:
            NTT2 = CAP // 256
            sc_total = S.cnt.get("sg", 0)
            wait("gpsimd", "sg", sc_total)
            it = 0
            for e_i in range(NE):
                ep = e_i % 2
                xsem = "sxl" if ep == 0 else "sxm"
                if e_i == 0:
                    op("gpsimd", "dma_start",
                       out=xgl[0][:].rearrange("p (b d) -> p b d", b=NBL),
                       in_=xg[0:CAP, :].rearrange("(b p) d -> p b d", p=128),
                       inc="sxl", inc_n=16)
                if e_i + 1 < NE:
                    op("gpsimd", "dma_start",
                       out=xgl[1 - ep][:].rearrange("p (b d) -> p b d", b=NBL),
                       in_=xg[(e_i + 1) * CAP:(e_i + 2) * CAP, :].rearrange(
                           "(b p) d -> p b d", p=128),
                       inc=("sxm" if ep == 0 else "sxl"), inc_n=16)
                wait("tensor", xsem)
                for b in range(NBL):
                    for k in range(NK):
                        op("tensor", "transpose", psT[:, k * 128:(k + 1) * 128],
                           in_=xgl[ep][:, b * D + k * 128:b * D + (k + 1) * 128],
                           identity=identB[:], inc=("sp" if k == NK - 1 else None))
                    gd = xgT[:].rearrange("p (k t) -> p k t", k=NK)[
                        :, :, b * 128:(b + 1) * 128]
                    gs = psT[:].rearrange("p (k t) -> p k t", k=NK)
                    vop("tensor_copy", gd[:, 0:NK // 2, :], gs[:, 0:NK // 2, :])
                    aop("copy", gd[:, NK // 2:NK, :], gs[:, NK // 2:NK, :])

                for fb in range(NFB):
                    par = it % 2
                    wait("gpsimd", "sp", wg_free[par])
                    wait("gpsimd", "sw")
                    for k in range(NK):
                        op("gpsimd", "dma_start",
                           out=WguB[par][:, k * 1024:(k + 1) * 1024].rearrange(
                               "p (h c) -> p h c", h=2),
                           in_=wgu5[e_i, fb, k * 128:(k + 1) * 128, :, :],
                           inc="sw", inc_n=16)
                    op("gpsimd", "dma_start",
                       out=WdnB[par][:].rearrange("p (f d) -> p f d", f=NFC),
                       in_=wdn4[e_i, :, fb * NFC:(fb + 1) * NFC, :],
                       inc="sw", inc_n=16)
                    w_cnt = S.cnt["sw"]
                    wait("tensor", "sw", w_cnt)

                    for tt in range(NTT2):
                        for fc in range(NFC):
                            bank = psA if fc % 2 == 0 else psB
                            for k in range(NK):
                                op("tensor", "matmul", bank[:, 0:256],
                                   lhsT=WguB[par][:, k * 1024 + fc * 128:
                                                  k * 1024 + (fc + 1) * 128],
                                   rhs=xgT[:, k * CAP + tt * 256:
                                           k * CAP + (tt + 1) * 256],
                                   start=(k == 0), stop=(k == NK - 1),
                                   inc=("sp" if k == NK - 1 else None))
                            for k in range(NK):
                                op("tensor", "matmul", bank[:, 256:512],
                                   lhsT=WguB[par][:, k * 1024 + 512 + fc * 128:
                                                  k * 1024 + 512 + (fc + 1) * 128],
                                   rhs=xgT[:, k * CAP + tt * 256:
                                           k * CAP + (tt + 1) * 256],
                                   start=(k == 0), stop=(k == NK - 1),
                                   inc=("sp" if k == NK - 1 else None))
                            sg_ = sgb[fc % 2]
                            aop("activation", sg_[:, 0:256], bank[:, 0:256], AF.Silu)
                            vop("tensor_mul", actb[fc][:, 0:256],
                                sg_[:, 0:256], bank[:, 256:512])
                        dbanks = (psC, psD, psW, psV)
                        for gi in range(4):
                            tok, dh = gi // 2, gi % 2
                            dbank = dbanks[gi]
                            ysl = yacc[gi][:, tt * 512:(tt + 1) * 512]
                            for fc in range(NFC):
                                last = (fc == NFC - 1) and fb == 0
                                op("tensor", "matmul", dbank[:],
                                   lhsT=actb[fc][:, tok * 128:(tok + 1) * 128],
                                   rhs=WdnB[par][:, fc * 1024 + dh * 512:
                                                 fc * 1024 + (dh + 1) * 512],
                                   start=(fc == 0), stop=last,
                                   inc=("sp" if last else None))
                            if fb > 0:
                                op("tensor", "matmul", dbank[:], lhsT=identB[:],
                                   rhs=ysl, start=False, stop=True, inc="sp")
                            if gi % 2 == 0:
                                vop("tensor_copy", ysl, dbank[:])
                            else:
                                aop("copy", ysl, dbank[:])
                    wg_free[par] = S.cnt["sp"]
                    it += 1
                for gi2 in range(4):
                    tok2, dh2 = gi2 // 2, gi2 % 2
                    op("gpsimd", "dma_start",
                       out=yout[e_i * CAP:(e_i + 1) * CAP, :].rearrange(
                           "(t b p) d -> p t b d", p=128, b=2)[
                           :, :, tok2, dh2 * 512:(dh2 + 1) * 512],
                       in_=yacc[gi2][:].rearrange("p (t d) -> p t d", d=512),
                       inc="sy", inc_n=16)
                for gi2 in range(4):
                    S._mark(S.readers, yacc[gi2].name, "sy", S.cnt["sy"])
            st_total = S.cnt["sy"]

            wait("gpsimd", "sy", st_total)
            gcnt = {0: st_total, 1: S.cnt.get("sg", 0)}
            for tc in range(n_own):
                for r in range(2):
                    gsem = "sy" if r == 0 else "sg"
                    op("gpsimd", "indirect_dma_start", out=ybk[r],
                       out_offset=None, in_=yout,
                       in_offset=IndirectOffsetOnAxis(
                           idxs[:, 2 * tc + r:2 * tc + r + 1], 0),
                       inc=gsem, inc_n=16)
                    wait("vector", gsem)
                    vop("scalar_tensor_tensor", acc[:, tc * D:(tc + 1) * D],
                        ybk[r], groutes[:, 2 * tc + r:2 * tc + r + 1],
                        acc[:, tc * D:(tc + 1) * D], ALU.mult, ALU.add)

        wait("sync", "sv")
        op("sync", "dma_start", out=out.rearrange("(i p) d -> p i d", p=128),
           in_=acc[:].rearrange("p (i d) -> p i d", d=D), inc="so", inc_n=16)
        wait("sync", "so")
        wait("gpsimd", "so")

        @block.sync
        def _(e):
            for fn in S.q["sync"]:
                fn(e)

        @block.gpsimd
        def _(e):
            for fn in S.q["gpsimd"]:
                fn(e)

        @block.tensor
        def _(e):
            for fn in S.q["tensor"]:
                fn(e)

        @block.vector
        def _(e):
            for fn in S.q["vector"]:
                fn(e)

        @block.scalar
        def _(e):
            for fn in S.q["scalar"]:
                fn(e)

    return nc


_NC_CACHE = None


def kernel(L0_attn_x, L0_attn_Wq, L0_attn_Wk, L0_attn_Wv, L0_attn_Wbeta,
           L0_attn_b_beta, L0_attn_Wout, L0_ffn_w_gate, L0_ffn_w_gate_up,
           L0_ffn_w_down):
    global _NC_CACHE
    x = np.ascontiguousarray(np.asarray(L0_attn_x, dtype=np.float32))
    common = dict(
        wq=np.ascontiguousarray(L0_attn_Wq, dtype=np.float32),
        wk=np.ascontiguousarray(L0_attn_Wk, dtype=np.float32),
        wv=np.ascontiguousarray(L0_attn_Wv, dtype=np.float32),
        wout=np.ascontiguousarray(L0_attn_Wout, dtype=np.float32),
        wbeta=np.ascontiguousarray(L0_attn_Wbeta, dtype=np.float32),
        bbeta=np.ascontiguousarray(L0_attn_b_beta, dtype=np.float32),
        wgate=np.ascontiguousarray(L0_ffn_w_gate, dtype=np.float32),
        wgu=np.ascontiguousarray(L0_ffn_w_gate_up, dtype=np.float32),
        wdn=np.ascontiguousarray(L0_ffn_w_down, dtype=np.float32),
        cident=np.eye(128, dtype=np.float32),
        cexp=np.tile((np.arange(NE) * CAP).astype(np.float32), (128, 1)),
        cslmask=np.tril(np.ones((128, 128), np.float32), -1),
        csumask=np.triu(np.ones((128, 128), np.float32), 0),
    )
    in_maps = []
    for c in range(8):
        b, h = c >> 1, c & 1
        if h == 0:
            xs = np.concatenate([np.zeros((2048, D), np.float32), x[b, 0:2048]], 0)
        else:
            xs = x[b]
        m = dict(common)
        m["xpad"] = np.ascontiguousarray(xs)
        in_maps.append(m)

    if _NC_CACHE is None:
        _NC_CACHE = build_program()
    res = run_bass_kernel_spmd(_NC_CACHE, in_maps, core_ids=list(range(8)))
    outp = np.empty((4, 4096, D), dtype=np.float32)
    for c in range(8):
        b, h = c >> 1, c & 1
        outp[b, h * 2048:(h + 1) * 2048, :] = res.results[c]["out"]
    return outp


# version 10
# speedup vs baseline: 1.1798x; 1.0332x over previous
import sys
import numpy as np
from contextlib import ExitStack

sys.path.insert(0, "/opt/trn_rl_repo")
import concourse.bass as bass
import concourse.mybir as mybir
from concourse.bass_utils import run_bass_kernel_spmd
from concourse.bass import IndirectOffsetOnAxis

F32 = mybir.dt.float32
BF16 = mybir.dt.bfloat16
AF = mybir.ActivationFunctionType
ALU = mybir.AluOpType
AX = mybir.AxisListType

D = 1024
NH = 16
DH = 64
NE = 8
FF = 4096
NCH = 32
OWN0 = 16
CH = 128
NK = D // 128
NFB = 8
NFC = 4
TT = 512
CAP = 768
SPARSE = True
PHASES = "AB"

ENGINES = ("sync", "gpsimd", "tensor", "vector", "scalar")
_ENGSEM = {"vector": "sv", "scalar": "sa"}
_PSUM = {"psA", "psB", "psC", "psD", "psW", "psV", "psT", "psU"}


class Sched:

    def __init__(self, sems):
        self.q = {e: [] for e in ENGINES}
        self.cnt = {}
        self.waited = {e: {} for e in ENGINES}
        self.sems = sems
        self.dirty = {}
        self.readers = {}
        self.psum_ready = {}
        self.pe_pending_w = set()
        self.pe_pending_r = set()

    def _mark(self, table, nm, sem, c):
        d = table.setdefault(nm, {})
        if d.get(sem, -1) < c:
            d[sem] = c

    def op(self, eng, method, *args, inc=None, inc_n=1, **kw):
        aps = [a for a in args if hasattr(a, "tensor")]
        for v in kw.values():
            if hasattr(v, "tensor"):
                aps.append(v)
            elif hasattr(v, "ap") and hasattr(v.ap, "tensor"):
                aps.append(v.ap)
        outa = kw.get("out")
        if outa is None and args and hasattr(args[0], "tensor"):
            outa = args[0]
        names = [(a.tensor.name, a is outa) for a in aps]
        for nm, _ in names:
            for sem, c in self.dirty.get(nm, {}).items():
                self.wait(eng, sem, c)
            if nm in _PSUM and eng != "tensor":
                pr = self.psum_ready.get(nm)
                if pr is not None:
                    self.wait(eng, "sp", pr)
        for nm, is_out in names:
            if is_out:
                assert nm not in self.pe_pending_r or eng == "tensor", (
                    f"write to {nm} while a PE read is not yet fenced")
                for sem, c in self.readers.get(nm, {}).items():
                    self.wait(eng, sem, c)
        if inc is not None:
            self.cnt[inc] = self.cnt.get(inc, 0) + inc_n
            sem, n = self.sems[inc], inc_n

            def fn(e, _m=method, _a=args, _k=kw, _s=sem, _n=n):
                getattr(e, _m)(*_a, **_k).then_inc(_s, _n)
        else:
            def fn(e, _m=method, _a=args, _k=kw):
                getattr(e, _m)(*_a, **_k)
        self.q[eng].append(fn)
        if eng == "tensor":
            for nm, is_out in names:
                if is_out:
                    self.pe_pending_w.add(nm)
                else:
                    self.pe_pending_r.add(nm)
            if inc is not None:
                c = self.cnt[inc]
                for nm in self.pe_pending_w:
                    self.psum_ready[nm] = c
                self.pe_pending_w.clear()
                for nm in self.pe_pending_r:
                    self._mark(self.readers, nm, "sp", c)
                self.pe_pending_r.clear()
        elif eng in _ENGSEM:
            assert inc == _ENGSEM[eng], f"writer on {eng} must inc: {method}"
            c = self.cnt[inc]
            for nm, is_out in names:
                if is_out:
                    self._mark(self.dirty, nm, _ENGSEM[eng], c)
                else:
                    self._mark(self.readers, nm, _ENGSEM[eng], c)
        elif inc is not None:
            c = self.cnt[inc]
            for nm, is_out in names:
                if not is_out:
                    self._mark(self.readers, nm, inc, c)
        return self.cnt.get(inc, 0) if inc else None

    def wait(self, eng, sem, val=None):
        if val is None:
            val = self.cnt.get(sem, 0)
        if val <= 0 or self.waited[eng].get(sem, -1) >= val:
            return
        self.waited[eng][sem] = val
        h = self.sems[sem]
        self.q[eng].append(lambda e, _h=h, _v=val: e.wait_ge(_h, _v))


def build_program():
    nc = bass.Bass("TRN2", target_bir_lowering=False)
    n_own = NCH - OWN0
    TT_ = min(TT, n_own * CH)
    NTT = (n_own * CH) // TT_
    ntok = TT_ // 128

    xpad = nc.dram_tensor("xpad", [NCH * CH, D], F32, kind="ExternalInput").ap()
    wq = nc.dram_tensor("wq", [D, D], F32, kind="ExternalInput").ap()
    wk = nc.dram_tensor("wk", [D, D], F32, kind="ExternalInput").ap()
    wv = nc.dram_tensor("wv", [D, D], F32, kind="ExternalInput").ap()
    wout = nc.dram_tensor("wout", [D, D], F32, kind="ExternalInput").ap()
    wbeta = nc.dram_tensor("wbeta", [D, NH], F32, kind="ExternalInput").ap()
    bbeta = nc.dram_tensor("bbeta", [NH], F32, kind="ExternalInput").ap()
    wgate = nc.dram_tensor("wgate", [D, NE], F32, kind="ExternalInput").ap()
    wgu = nc.dram_tensor("wgu", [NE, D, 2 * FF], F32, kind="ExternalInput").ap()
    wdn = nc.dram_tensor("wdn", [NE, FF, D], F32, kind="ExternalInput").ap()
    cident = nc.dram_tensor("cident", [128, 128], F32, kind="ExternalInput").ap()
    cslmask = nc.dram_tensor("cslmask", [128, 128], F32, kind="ExternalInput").ap()
    csumask = nc.dram_tensor("csumask", [128, 128], F32, kind="ExternalInput").ap()
    cexp = nc.dram_tensor("cexp", [128, NE], F32, kind="ExternalInput").ap()
    xg = nc.dram_tensor("xg", [NE * CAP, D], BF16, kind="Internal").ap()
    yout = nc.dram_tensor("yout", [NE * CAP, D], BF16, kind="Internal").ap()
    out = nc.dram_tensor("out", [n_own * CH, D], F32, kind="ExternalOutput").ap()

    with ExitStack() as ctx:
        sb = lambda n, s, d: ctx.enter_context(nc.sbuf_tensor(n, s, d))
        psd = lambda n, s, d: ctx.enter_context(nc.psum_tensor(n, s, d))

        SHARED = 106816
        shared = sb("shared", [128, SHARED], mybir.dt.uint8)
        shared_base = nc.lookup_mloc(shared).addr
        _roff = [0]

        def _ralloc(name, shape, dtype):
            per = 1
            for d_ in shape[1:]:
                per *= d_
            per *= mybir.dt.size(dtype)
            h = nc.alloc_sbuf_tensor_at(name, shape, dtype,
                                        offset=shared_base + _roff[0])
            _roff[0] = (_roff[0] + per + 31) // 32 * 32
            assert _roff[0] <= SHARED, f"shared region overflow at {name}"
            return h

        xnT_own = sb("xnT_own", [128, NK * n_own * CH], BF16)
        acc = sb("acc", [128, n_own * D], F32)
        gates = None if SPARSE else sb("gates", [128, n_own * NE], F32)
        M32 = sb("M32", [64, NH * DH], F32)
        M16 = sb("M16", [64, NH * DH], BF16)
        idxs = sb("idxs", [128, 2 * (NCH - OWN0)], mybir.dt.uint32)
        groutes = sb("groutes", [128, 2 * (NCH - OWN0)], F32)
        identB = sb("identB", [128, 128], BF16)
        identF = sb("identF", [16, 16], F32)
        slmask = sb("slmask", [128, 128], BF16)
        sumask = sb("sumask", [128, 128], BF16)

        WqB = _ralloc("WqB", [128, NK * D], BF16)
        WkB = _ralloc("WkB", [128, NK * D], BF16)
        WvB = _ralloc("WvB", [128, NK * D], BF16)
        WoB = _ralloc("WoB", [128, NK * D], BF16)
        WbB = _ralloc("WbB", [128, NK * NH], BF16)
        WgB = _ralloc("WgB", [128, NK * NE], BF16)
        bbn = _ralloc("bbn", [16, 1], F32)
        bbp = _ralloc("bbp", [16, 1], F32)
        xin = _ralloc("xin0", [128, D], F32)
        xnb = _ralloc("xnb", [128, D], BF16)
        KT = _ralloc("KTb", [128, 8 * 128], BF16)
        VT = _ralloc("VTb", [128, 8 * 128], BF16)
        QT = _ralloc("QTb", [128, 8 * 128], BF16)
        betaT = _ralloc("betaT", [16, 128], F32)
        bcols = _ralloc("bcols", [128, 16], F32)
        bcolsn = _ralloc("bcolsn", [128, 16], F32)
        zgs = _ralloc("zgs", [8, 128], F32)
        zt = _ralloc("ztb", [128, 8], F32)
        Ktok = [_ralloc(f"Ktok{g}", [128, 8 * DH], BF16) for g in range(2)]
        Kn = Ktok
        Vtb = [_ralloc(f"Vtb{g}", [128, 8 * DH], BF16) for g in range(2)]
        Rt = Vtb
        KnT = [_ralloc(f"KnT{g}", [64, 8 * 128], BF16) for g in range(2)]
        ssb = [_ralloc(f"ssb{g}", [128, 8], F32) for g in range(2)]
        invb = [_ralloc(f"invb{g}", [128, 8], F32) for g in range(2)]
        Gb = [_ralloc(f"Gb{g}", [128, 8 * 128], BF16) for g in range(2)]
        Qh0 = Gb
        Gp = [_ralloc(f"Gp{g}", [128, 8 * 128], BF16) for g in range(2)]
        Pk = Gb
        Pq = Gp
        Sa = [_ralloc(f"Sa{g}", [128, 8 * 128], BF16) for g in range(2)]
        Xg = [_ralloc(f"Xg{g}", [128, 8 * DH], BF16) for g in range(2)]
        Lg = Sa
        Oall = [_ralloc(f"Oall{g}", [128, 8 * DH], BF16) for g in range(2)]
        Qtok = Oall
        OTb = _ralloc("OTb", [128, 8 * 128], BF16)
        c_s1 = _ralloc("c_s1", [128, 1], F32)
        c_mean = _ralloc("c_mean", [128, 1], F32)
        c_s2 = _ralloc("c_s2", [128, 1], F32)
        c_m2 = _ralloc("c_m2", [128, 1], F32)
        c_var = _ralloc("c_var", [128, 1], F32)
        c_rstd = _ralloc("c_rstd", [128, 1], F32)
        g_nm = _ralloc("g_nm", [128, 1], F32)
        g_ex = _ralloc("g_ex", [128, 8], F32)
        g_sum = _ralloc("g_sum", [128, 1], F32)
        g_rs = _ralloc("g_rs", [128, 1], F32)
        g_p = _ralloc("g_p", [128, 8], F32)
        g_m1 = _ralloc("g_m1", [128, 1], F32)
        g_m2 = _ralloc("g_m2", [128, 1], F32)
        g_mv = _ralloc("g_mv", [128, 8], F32)
        g_pm1 = _ralloc("g_pm1", [128, 8], F32)
        g_pm2 = _ralloc("g_pm2", [128, 8], F32)
        g_ne = _ralloc("g_ne", [128, 8], F32)
        g_den = _ralloc("g_den", [128, 1], F32)
        r_cnt8 = sb("r_cnt8", [8, 1], F32)
        r_tmp8 = sb("r_tmp8", [8, 1], F32)
        r_snzb = sb("r_snzb", [128, 8], BF16)
        r_m1 = sb("r_m1", [128, 8], F32)
        r_m2 = sb("r_m2", [128, 8], F32)
        r_t = sb("r_t", [128, 8], F32)
        r_o1 = sb("r_o1", [128, 1], F32)
        r_o2 = sb("r_o2", [128, 1], F32)
        capexp = sb("capexp", [128, NE], F32)
        A_BYTES = _roff[0]
        import os
        if os.environ.get("PRINT_AB"): print("A_BYTES", A_BYTES)

        _roff[0] = 0
        WguB = [_ralloc("WguB0", [128, NK * 1024], BF16),
                _ralloc("WguB1", [128, NK * 1024], BF16)]
        WdnB = [_ralloc("WdnB0", [128, NFC * 1024], BF16),
                _ralloc("WdnB1", [128, NFC * 1024], BF16)]
        actb = [_ralloc(f"actb{i}", [128, 512], BF16) for i in range(NFC)]
        sgb = [_ralloc("sgb0", [128, 256], F32), _ralloc("sgb1", [128, 256], F32)]
        NBL = CAP // 128
        xgl = [_ralloc("xgl0", [128, NBL * D], BF16),
               _ralloc("xgl1", [128, NBL * D], BF16)]
        xgT = _ralloc("xgT", [128, NK * CAP], BF16)
        yacc = [_ralloc(f"yacc{i}", [128, (NBL // 2) * 512], BF16) for i in range(4)]
        ybk = [xgl[0][:, 0:D], xgl[0][:, D:2 * D]]
        if os.environ.get("PRINT_AB"): print("B_BYTES", _roff[0])

        psA = psd("psA", [128, 512], F32)
        psB = psd("psB", [128, 512], F32)
        psC = psd("psC", [128, 512], F32)
        psD = psd("psD", [128, 512], F32)
        psW = psd("psW", [128, 512], F32)
        psV = psd("psV", [128, 512], F32)
        psT = psd("psT", [128, 1024], BF16)
        psU = psd("psU", [128, 1024], BF16)
        fpair = [(psA, psB), (psC, psD)]
        trb = [psT, psU]
        kb = [psA, psC]

        sems = {}
        for name in ("sc", "si", "sxe", "sxo", "swp", "sw", "so",
                     "sg", "sy", "sxl", "sxm", "sp", "sv", "sa", "sw2",
                     "gb0", "gb1", "gb2", "gb3", "gb4", "gb5"):
            sems[name] = ctx.enter_context(nc.semaphore(name))
        S = Sched(sems)
        block = ctx.enter_context(nc.Block())
        op, wait = S.op, S.wait

        def vop(method, *args, **kw):
            op("vector", method, *args, inc="sv", **kw)

        def aop(method, *args, **kw):
            op("scalar", method, *args, inc="sa", **kw)

        def F32T(g, hl):
            bank = fpair[g][0] if hl < 4 else fpair[g][1]
            s_ = hl % 4
            return bank[:, s_ * 128:(s_ + 1) * 128]

        op("sync", "dma_start", out=identF[:], in_=cident[0:16, 0:16], inc="sc", inc_n=16)
        op("gpsimd", "dma_start", out=slmask[:], in_=cslmask, inc="swp", inc_n=16)
        op("gpsimd", "dma_start", out=sumask[:], in_=csumask, inc="swp", inc_n=16)
        op("sync", "dma_start", out=bbp[:],
           in_=bbeta.rearrange("(p o) -> p o", o=1), inc="sc", inc_n=16)
        op("sync", "dma_start",
           out=acc[:].rearrange("p (i d) -> p i d", d=D),
           in_=xpad[OWN0 * CH:NCH * CH, :].rearrange("(i p) d -> p i d", p=128),
           inc="si", inc_n=16)

        op("gpsimd", "dma_start", out=identB[:], in_=cident, inc="swp", inc_n=16)
        for wsrc, wdst in ((wq, WqB), (wk, WkB), (wv, WvB), (wout, WoB)):
            for k in range(NK):
                op("gpsimd", "dma_start", out=wdst[:, k * D:(k + 1) * D],
                   in_=wsrc[k * 128:(k + 1) * 128, :], inc="swp", inc_n=16)
        op("gpsimd", "dma_start",
           out=WbB[:].rearrange("p (k h) -> p k h", k=NK),
           in_=wbeta.rearrange("(k p) h -> p k h", p=128), inc="swp", inc_n=16)
        op("gpsimd", "dma_start",
           out=WgB[:].rearrange("p (k h) -> p k h", k=NK),
           in_=wgate.rearrange("(k p) h -> p k h", p=128), inc="swp", inc_n=16)
        proj_w = S.cnt["swp"]

        op("sync", "dma_start", out=capexp[:], in_=cexp, inc="sc", inc_n=16)
        wait("vector", "sc")
        if SPARSE:
            vop("memset", xnb[:], 0.0)
            wait("gpsimd", "sv")
            for b in range(NE * CAP // 128):
                op("gpsimd", "dma_start",
                   out=xg[b * 128:(b + 1) * 128, :], in_=xnb[:],
                   inc="sg", inc_n=16)
        vop("memset", r_cnt8[:], 0.0)
        vop("tensor_scalar_mul", bbn[:], bbp[:], -1.0)
        vop("memset", M32[:], 0.0)
        vop("memset", M16[:], 0.0)

        wait("tensor", "sc")
        wait("tensor", "swp", proj_w)
        def emit_ln(c):
            sxp = "sxe" if c % 2 == 0 else "sxo"
            op("sync", "dma_start", out=xin[:], in_=xpad[c * CH:(c + 1) * CH, :],
               inc=sxp, inc_n=16)
            wait("vector", sxp)
            wait("vector", "sg")
            vop("reduce_sum", c_s1[:], xin[:], axis=AX.X)
            vop("tensor_scalar_mul", c_mean[:], c_s1[:], 1.0 / D)
            vop("tensor_mul", xnb[:], xin[:], xin[:])
            vop("reduce_sum", c_s2[:], xnb[:], axis=AX.X)
            vop("tensor_mul", c_m2[:], c_mean[:], c_mean[:])
            vop("scalar_tensor_tensor", c_var[:], c_s2[:], 1.0 / D, c_m2[:],
                ALU.mult, ALU.subtract)
            vop("tensor_scalar_add", c_var[:], c_var[:], 1e-5)
            aop("activation", c_rstd[:], c_var[:], AF.Ln)
            aop("activation", c_rstd[:], c_rstd[:], AF.Exp, scale=-0.5)
            vop("tensor_scalar", xnb[:], xin[:], c_mean[:], c_rstd[:],
                ALU.subtract, ALU.mult)

        def front_pieces(c):
            own = c >= OWN0
            tc = c - OWN0
            slot = tc if own else (c & 1)

            def XN(k, _slot=slot):
                base = k * (n_own * CH) + _slot * CH
                return xnT_own[:, base:base + CH]

            def p_xnt():
                for i in range(NK):
                    op("tensor", "transpose", psT[:, i * 128:(i + 1) * 128],
                       in_=xnb[:, i * 128:(i + 1) * 128], identity=identB[:],
                       inc=("sp" if i == NK - 1 else None))
                xd = xnT_own[:].rearrange("p (k t) -> p k t", k=NK)[
                    :, :, slot * CH:(slot + 1) * CH]
                aop("copy", xd, psT[:].rearrange("p (k t) -> p k t", k=NK))

            def mk_proj(wbuf, dst, half, bank):
                def p():
                    for i in range(4):
                        ii = half * 4 + i
                        for k in range(NK):
                            op("tensor", "matmul", bank[:, i * 128:(i + 1) * 128],
                               lhsT=wbuf[:, k * D + ii * 128:k * D + (ii + 1) * 128],
                               rhs=XN(k), start=(k == 0), stop=(k == NK - 1),
                               inc=("sp" if (i == 3 and k == NK - 1) else None))
                    aop("copy", dst[:, half * 512:(half + 1) * 512], bank[:])
                return p

            def p_beta_mm():
                for k in range(NK):
                    op("tensor", "matmul", psV[0:16, 0:128],
                       lhsT=WbB[:, k * NH:(k + 1) * NH], rhs=XN(k),
                       start=(k == 0), stop=(k == NK - 1),
                       inc=("sp" if k == NK - 1 else None))
                aop("activation", betaT[:], psV[0:16, 0:128], AF.Exp,
                    scale=-1.0, bias=bbn[:])
                vop("tensor_scalar_add", betaT[:], betaT[:], 1.0)
                vop("reciprocal", betaT[:], betaT[:])

            def p_beta_fin():
                op("tensor", "transpose", psV[:, 288:304], in_=betaT[:],
                   identity=identF[:], inc="sp")
                vop("tensor_copy", bcols[:], psV[:, 288:304])
                vop("tensor_scalar_mul", bcolsn[:], bcols[:], -1.0)

            def p_gate_mm():
                for k in range(NK):
                    op("tensor", "matmul", psV[0:8, 128:256],
                       lhsT=WgB[:, k * NE:(k + 1) * NE], rhs=XN(k),
                       start=(k == 0), stop=(k == NK - 1),
                       inc=("sp" if k == NK - 1 else None))
                vop("tensor_copy", zgs[:], psV[0:8, 128:256])

            def p_gate_fin():
                op("tensor", "transpose", psV[:, 304:312], in_=zgs[:],
                   identity=identF[0:8, 0:8], inc="sp")
                vop("tensor_copy", zt[:], psV[:, 304:312])
                vop("reduce_max", g_nm[:], zt[:], axis=AX.X, negate=True)
                aop("activation", g_ex[:], zt[:], AF.Exp, bias=g_nm[:])
                vop("reduce_sum", g_sum[:], g_ex[:], axis=AX.X)
                vop("reciprocal", g_rs[:], g_sum[:])
                vop("tensor_scalar_mul", g_p[:], g_ex[:], g_rs[:])
                vop("reduce_max", g_m1[:], g_p[:], axis=AX.X)
                vop("memset", g_mv[:], -1e30)
                vop("tensor_copy", g_mv[:, 0:1], g_m1[:])
                vop("match_replace", g_pm1[:], in_to_replace=g_mv[:],
                    in_values=g_p[:], imm_value=-1.0)
                vop("reduce_max", g_m2[:], g_pm1[:], axis=AX.X)
                vop("tensor_copy", g_mv[:, 0:1], g_m2[:])
                vop("match_replace", g_pm2[:], in_to_replace=g_mv[:],
                    in_values=g_pm1[:], imm_value=-1.0)
                vop("tensor_tensor", g_ne[:], g_p[:], g_pm2[:], ALU.not_equal)
                vop("tensor_add", g_den[:], g_m1[:], g_m2[:])
                vop("tensor_scalar_max", g_den[:], g_den[:], 1e-9)
                vop("reciprocal", g_den[:], g_den[:])
                if not SPARSE:
                    vop("tensor_mul", g_ne[:], g_ne[:], g_p[:])
                    vop("tensor_scalar", gates[:, tc * NE:(tc + 1) * NE],
                        g_ne[:], g_den[:], None, ALU.mult)

            def p_route():
                vop("tensor_tensor", r_m1[:], g_p[:], g_pm1[:], ALU.not_equal)
                vop("tensor_tensor", r_m2[:], g_pm1[:], g_pm2[:], ALU.not_equal)
                vop("tensor_add", r_t[:], r_m1[:], r_m2[:])
                vop("tensor_copy", r_snzb[:], r_t[:])

            def p_route_b():
                op("tensor", "transpose", psT[0:8, 0:128], in_=r_snzb[:],
                   identity=identB[:], inc="sp")
                vop("tensor_copy", zgs[:], psT[0:8, 0:128])
                vop("tensor_tensor_scan", betaT[0:8, :], zgs[:],
                    zgs[:], 0.0, ALU.add, ALU.max)
                vop("tensor_scalar", betaT[0:8, :], betaT[0:8, :], r_cnt8[:], -1.0,
                    ALU.add, ALU.add)
                vop("reduce_sum", r_tmp8[:], zgs[:], axis=AX.X)
                vop("tensor_add", r_cnt8[:], r_cnt8[:], r_tmp8[:])

            def p_route_c():
                op("tensor", "transpose", psV[:, 312:320], in_=betaT[0:8, :],
                   identity=identF[0:8, 0:8], inc="sp")
                vop("tensor_tensor", r_t[:], psV[:, 312:320], capexp[:], ALU.add)
                vop("tensor_mul", g_ex[:], r_t[:], r_m1[:])
                vop("reduce_sum", r_o1[:], g_ex[:], axis=AX.X)
                vop("tensor_mul", g_ex[:], r_t[:], r_m2[:])
                vop("reduce_sum", r_o2[:], g_ex[:], axis=AX.X)
                vop("tensor_scalar_min", r_o1[:], r_o1[:], float(NE * CAP - 1))
                vop("tensor_scalar_min", r_o2[:], r_o2[:], float(NE * CAP - 1))
                vop("tensor_copy", idxs[:, 2 * tc:2 * tc + 1], r_o1[:])
                vop("tensor_copy", idxs[:, 2 * tc + 1:2 * tc + 2], r_o2[:])
                vop("tensor_mul", groutes[:, 2 * tc:2 * tc + 1], g_m1[:], g_den[:])
                vop("tensor_mul", groutes[:, 2 * tc + 1:2 * tc + 2], g_m2[:], g_den[:])
                op("gpsimd", "indirect_dma_start", out=xg,
                   out_offset=IndirectOffsetOnAxis(idxs[:, 2 * tc:2 * tc + 1], 0),
                   in_=xnb[:], in_offset=None, inc="sg", inc_n=16)
                op("gpsimd", "indirect_dma_start", out=xg,
                   out_offset=IndirectOffsetOnAxis(idxs[:, 2 * tc + 1:2 * tc + 2], 0),
                   in_=xnb[:], in_offset=None, inc="sg", inc_n=16)

            pieces = [p_xnt,
                      mk_proj(WkB, KT, 0, psW), mk_proj(WkB, KT, 1, psV),
                      mk_proj(WvB, VT, 0, psW), mk_proj(WvB, VT, 1, psV)]
            if own:
                pieces += [mk_proj(WqB, QT, 0, psW), mk_proj(WqB, QT, 1, psV)]
            pieces += [p_beta_mm, p_beta_fin]
            if own:
                pieces += [p_gate_mm, p_gate_fin]
                if SPARSE:
                    pieces += [p_route, p_route_b, p_route_c]
            return pieces

        def chunk_steps(c):
            own = c >= OWN0
            tc = c - OWN0

            def PTile(buf, g, p):
                t = g * 4 + p
                return buf[:, t * 128:(t + 1) * 128]

            def BC(g, hl):
                j = g * 8 + hl
                return bcols[:, j:j + 1]

            def BCN(g, hl):
                j = g * 8 + hl
                return bcolsn[:, j:j + 1]

            def cop(g, *args):
                if g == 0:
                    vop("tensor_copy", *args)
                else:
                    aop("copy", *args)

            def s_t1(g):
                tr = trb[g]
                for p in range(4):
                    op("tensor", "transpose", tr[:, p * 128:(p + 1) * 128],
                       in_=PTile(KT, g, p), identity=identB[:],
                       inc=("sp" if p == 3 else None))
                cop(g, Ktok[g][:], tr[:, 0:512])

            def s_t5a(g):
                tr = trb[g]
                for p in range(4):
                    op("tensor", "transpose", tr[:, p * 128:(p + 1) * 128],
                       in_=PTile(VT, g, p), identity=identB[:],
                       inc=("sp" if p == 3 else None))
                for hl in range(8):
                    pp, b = hl // 2, 64 * (hl % 2)
                    if g == 0:
                        vop("tensor_scalar_mul", Vtb[g][:, hl * DH:(hl + 1) * DH],
                            tr[:, pp * 128 + b:pp * 128 + b + 64], BC(g, hl))
                    else:
                        aop("activation", Vtb[g][:, hl * DH:(hl + 1) * DH],
                            tr[:, pp * 128 + b:pp * 128 + b + 64], AF.Copy,
                            scale=BC(g, hl))

            def s_q1(g):
                tr = trb[g]
                for p in range(4):
                    op("tensor", "transpose", tr[:, p * 128:(p + 1) * 128],
                       in_=PTile(QT, g, p), identity=identB[:],
                       inc=("sp" if p == 3 else None))
                cop(g, Qtok[g][:], tr[:, 0:512])

            def s_t2(g):
                vop("tensor_mul", kb[g][:], Ktok[g][:], Ktok[g][:])
                vop("reduce_sum", ssb[g][:],
                    kb[g][:].rearrange("p (h d) -> p h d", h=8), axis=AX.X)
                vop("tensor_scalar_max", ssb[g][:], ssb[g][:], 1e-24)
                aop("activation", invb[g][:], ssb[g][:], AF.Ln)
                aop("activation", invb[g][:], invb[g][:], AF.Exp, scale=-0.5)
                for hl in range(8):
                    vop("tensor_scalar_mul", Kn[g][:, hl * DH:(hl + 1) * DH],
                        Ktok[g][:, hl * DH:(hl + 1) * DH], invb[g][:, hl:hl + 1])

            def s_t4(g):
                tr = trb[g]
                for hl in range(8):
                    op("tensor", "transpose", tr[0:64, hl * 128:(hl + 1) * 128],
                       in_=Kn[g][:, hl * DH:(hl + 1) * DH], identity=identB[:],
                       inc=("sp" if hl == 7 else None))
                cop(g, KnT[g][:], tr[0:64, :])

            def s_t5b(g):
                for hl in range(8):
                    hg = g * 8 + hl
                    op("tensor", "matmul", kb[g][:, hl * DH:(hl + 1) * DH],
                       lhsT=KnT[g][0:64, hl * 128:(hl + 1) * 128],
                       rhs=M16[0:64, hg * DH:(hg + 1) * DH],
                       start=True, stop=True, inc=("sp" if hl == 7 else None))
                for hl in range(8):
                    vop("scalar_tensor_tensor", Rt[g][:, hl * DH:(hl + 1) * DH],
                        kb[g][:, hl * DH:(hl + 1) * DH], BCN(g, hl),
                        Vtb[g][:, hl * DH:(hl + 1) * DH], ALU.mult, ALU.add)

            def s_t6(g):
                for hl in range(8):
                    op("tensor", "matmul", F32T(g, hl),
                       lhsT=KnT[g][0:64, hl * 128:(hl + 1) * 128],
                       rhs=KnT[g][0:64, hl * 128:(hl + 1) * 128],
                       start=True, stop=True, inc=("sp" if hl == 7 else None))
                for hl in range(8):
                    vop("scalar_tensor_tensor", Gb[g][:, hl * 128:(hl + 1) * 128],
                        F32T(g, hl), BC(g, hl), slmask[:], ALU.mult, ALU.mult)

            def s_t7(g):
                tr = trb[g]
                for hl in range(8):
                    op("tensor", "transpose", tr[:, hl * 128:(hl + 1) * 128],
                       in_=Gb[g][:, hl * 128:(hl + 1) * 128], identity=identB[:],
                       inc=("sp" if hl == 7 else None))
                cop(g, Gp[g][:], tr[:])
                for hl in range(8):
                    vop("tensor_sub", Sa[g][:, hl * 128:(hl + 1) * 128],
                        identB[:], Gp[g][:, hl * 128:(hl + 1) * 128])

            def s_lev_p(g, lev):
                for hl in range(8):
                    sl_ = slice(hl * 128, (hl + 1) * 128)
                    op("tensor", "matmul", F32T(g, hl),
                       lhsT=Pq[g][:, sl_], rhs=Pk[g][:, sl_],
                       start=True, stop=True, inc=("sp" if hl == 7 else None))
                cop(g, Pk[g][:, 0:512], fpair[g][0][:])
                cop(g, Pk[g][:, 512:1024], fpair[g][1][:])

            def s_lev_pt(g, lev):
                tr = trb[g]
                for hl in range(8):
                    op("tensor", "transpose", tr[:, hl * 128:(hl + 1) * 128],
                       in_=Pk[g][:, hl * 128:(hl + 1) * 128], identity=identB[:],
                       inc=("sp" if hl == 7 else None))
                cop(g, Pq[g][:], tr[:])

            def s_lev_s(g, lev):
                for hl in range(8):
                    sl_ = slice(hl * 128, (hl + 1) * 128)
                    op("tensor", "matmul", F32T(g, hl), lhsT=Pk[g][:, sl_],
                       rhs=Sa[g][:, sl_], start=True, stop=False)
                    op("tensor", "matmul", F32T(g, hl), lhsT=identB[:],
                       rhs=Sa[g][:, sl_], start=False, stop=True,
                       inc=("sp" if hl == 7 else None))
                cop(g, Sa[g][:, 0:512], fpair[g][0][:])
                cop(g, Sa[g][:, 512:1024], fpair[g][1][:])

            def s_t9(g):
                for hl in range(8):
                    op("tensor", "matmul", kb[g][:, hl * DH:(hl + 1) * DH],
                       lhsT=Sa[g][:, hl * 128:(hl + 1) * 128],
                       rhs=Rt[g][:, hl * DH:(hl + 1) * DH],
                       start=True, stop=True, inc=("sp" if hl == 7 else None))
                cop(g, Xg[g][:], kb[g][:])

            def s_t10(g):
                for hl in range(8):
                    op("tensor", "matmul", kb[g][0:64, hl * DH:(hl + 1) * DH],
                       lhsT=Kn[g][:, hl * DH:(hl + 1) * DH],
                       rhs=Xg[g][:, hl * DH:(hl + 1) * DH],
                       start=True, stop=True, inc=("sp" if hl == 7 else None))
                for hl in range(8):
                    hg = g * 8 + hl
                    vop("tensor_tensor", M32[:, hg * DH:(hg + 1) * DH],
                        M32[:, hg * DH:(hg + 1) * DH],
                        kb[g][0:64, hl * DH:(hl + 1) * DH], ALU.add)

            def s_q2(g):
                tr = trb[g]
                for hl in range(8):
                    op("tensor", "transpose", tr[0:64, hl * 128:(hl + 1) * 128],
                       in_=Qtok[g][:, hl * DH:(hl + 1) * DH], identity=identB[:],
                       inc=("sp" if hl == 7 else None))
                cop(g, Qh0[g][0:64, :], tr[0:64, :])

            def s_kq(g):
                for hl in range(8):
                    op("tensor", "matmul", F32T(g, hl),
                       lhsT=KnT[g][0:64, hl * 128:(hl + 1) * 128],
                       rhs=Qh0[g][0:64, hl * 128:(hl + 1) * 128],
                       start=True, stop=True, inc=("sp" if hl == 7 else None))
                for hl in range(8):
                    vop("tensor_mul", Lg[g][:, hl * 128:(hl + 1) * 128],
                        F32T(g, hl), sumask[:])

            def s_o(g):
                for hl in range(8):
                    hg = g * 8 + hl
                    op("tensor", "matmul", kb[g][:, hl * DH:(hl + 1) * DH],
                       lhsT=Qh0[g][0:64, hl * 128:(hl + 1) * 128],
                       rhs=M16[0:64, hg * DH:(hg + 1) * DH],
                       start=True, stop=False)
                    op("tensor", "matmul", kb[g][:, hl * DH:(hl + 1) * DH],
                       lhsT=Lg[g][:, hl * 128:(hl + 1) * 128],
                       rhs=Xg[g][:, hl * DH:(hl + 1) * DH],
                       start=False, stop=True, inc=("sp" if hl == 7 else None))
                cop(g, Oall[g][:], kb[g][:])

            def s_ot(g):
                tr = trb[g]
                for p in range(4):
                    op("tensor", "transpose", tr[:, p * 128:(p + 1) * 128],
                       in_=Oall[g][:, p * 128:(p + 1) * 128], identity=identB[:],
                       inc=("sp" if p == 3 else None))
                cop(g, OTb[:, g * 512:(g + 1) * 512], tr[:, 0:512])

            base = [s_t1, s_t5a]
            if own:
                base.append(s_q1)
            base += [s_t2, s_t4, s_t5b, s_t6, s_t7]
            for lev in range(1, 7):
                base.append(lambda g, _l=lev: s_lev_p(g, _l))
                if lev < 6:
                    base.append(lambda g, _l=lev: s_lev_pt(g, _l))
                base.append(lambda g, _l=lev: s_lev_s(g, _l))
            base += [s_t9, s_t10]
            if own:
                base += [s_q2, s_kq, s_o, s_ot]
            units = []
            for st in base:
                units.append(lambda _s=st: _s(0))
                units.append(lambda _s=st: _s(1))
                if (own and st is s_o) or (not own and st is s_t10):
                    units.append(lambda: aop("copy", M16[:], M32[:]))

            if own:
                def p_wout():
                    for i in range(8):
                        bank = psC if i < 4 else psD
                        for k in range(NK):
                            op("tensor", "matmul",
                               bank[:, (i % 4) * 128:((i % 4) + 1) * 128],
                               lhsT=WoB[:, k * D + i * 128:k * D + (i + 1) * 128],
                               rhs=OTb[:, k * 128:(k + 1) * 128],
                               start=(k == 0), stop=(k == NK - 1),
                               inc=("sp" if (i == 7 and k == NK - 1) else None))
                    vop("tensor_copy", Lg[1][:, 0:512], psC[:])
                    vop("tensor_copy", Lg[1][:, 512:1024], psD[:])
                    for i in range(8):
                        op("tensor", "transpose", psT[:, i * 128:(i + 1) * 128],
                           in_=Lg[1][:, i * 128:(i + 1) * 128], identity=identB[:],
                           inc=("sp" if i == 7 else None))
                    wait("vector", "si", 16)
                    vop("tensor_tensor", acc[:, tc * D:(tc + 1) * D],
                        acc[:, tc * D:(tc + 1) * D], psT[:], ALU.add)
                units.append(p_wout)
            return units

        if "A" in PHASES:
            emit_ln(0)
            for p in front_pieces(0):
                p()
        for c in (range(NCH) if "A" in PHASES else []):
            units = chunk_steps(c)
            if c + 1 < NCH:
                nf = [lambda _c=c + 1: emit_ln(_c)] + front_pieces(c + 1)
            else:
                nf = []
            nu, nn = len(units), len(nf)
            k = 0
            for i, u in enumerate(units):
                u()
                while k < nn and (i + 1) * nn >= (k + 1) * nu:
                    nf[k]()
                    k += 1
            while k < nn:
                nf[k]()
                k += 1

        wait("tensor", "sv")
        wait("tensor", "sa")
        pa_sp = S.cnt.get("sp", 0)
        wait("gpsimd", "sp", pa_sp)
        wait("gpsimd", "sv")
        wait("gpsimd", "sa")
        wgu5 = wgu.rearrange("e r (h f c) -> e f r h c", h=2, f=NFB)
        wdn4 = wdn.rearrange("e (f p) d -> e p f d", p=128)

        wg_free = [pa_sp, pa_sp]
        it = 0
        for e_i in range(0 if (SPARSE or "B" not in PHASES) else NE):
            for fb in range(NFB):
                par = it % 2
                wait("gpsimd", "sp", wg_free[par])
                wait("gpsimd", "sw")
                for k in range(NK):
                    op("gpsimd", "dma_start",
                       out=WguB[par][:, k * 1024:(k + 1) * 1024].rearrange(
                           "p (h c) -> p h c", h=2),
                       in_=wgu5[e_i, fb, k * 128:(k + 1) * 128, :, :],
                       inc="sw", inc_n=16)
                op("gpsimd", "dma_start",
                   out=WdnB[par][:].rearrange("p (f d) -> p f d", f=NFC),
                   in_=wdn4[e_i, :, fb * NFC:(fb + 1) * NFC, :],
                   inc="sw", inc_n=16)
                w_cnt = S.cnt["sw"]

                wait("tensor", "sw", w_cnt)
                for tt in range(NTT):
                    for fc in range(NFC):
                        gbank = psA if fc % 2 == 0 else psC
                        ubank = psB if fc % 2 == 0 else psD
                        for k in range(NK):
                            op("tensor", "matmul", gbank[:, 0:TT_],
                               lhsT=WguB[par][:, k * 1024 + fc * 128:
                                              k * 1024 + (fc + 1) * 128],
                               rhs=xnT_own[:, k * (n_own * CH) + tt * TT_:
                                           k * (n_own * CH) + (tt + 1) * TT_],
                               start=(k == 0), stop=(k == NK - 1),
                               inc=("sp" if k == NK - 1 else None))
                        for k in range(NK):
                            op("tensor", "matmul", ubank[:, 0:TT_],
                               lhsT=WguB[par][:, k * 1024 + 512 + fc * 128:
                                              k * 1024 + 512 + (fc + 1) * 128],
                               rhs=xnT_own[:, k * (n_own * CH) + tt * TT_:
                                           k * (n_own * CH) + (tt + 1) * TT_],
                               start=(k == 0), stop=(k == NK - 1),
                               inc=("sp" if k == NK - 1 else None))
                        sg = sgb[fc % 2]
                        aop("activation", sg[:, 0:TT_], gbank[:, 0:TT_], AF.Silu)
                        vop("tensor_mul", actb[fc][:, 0:TT_],
                            sg[:, 0:TT_], ubank[:, 0:TT_])
                    for gi in range(2 * ntok):
                        tok, dh = gi // 2, gi % 2
                        dbank = psW if gi % 2 == 0 else psV
                        for fc in range(NFC):
                            op("tensor", "matmul", dbank[:],
                               lhsT=actb[fc][:, tok * 128:(tok + 1) * 128],
                               rhs=WdnB[par][:, fc * 1024 + dh * 512:
                                             fc * 1024 + (dh + 1) * 512],
                               start=(fc == 0), stop=(fc == NFC - 1),
                               inc=("sp" if fc == NFC - 1 else None))
                        tc2 = tt * ntok + tok
                        vop("scalar_tensor_tensor",
                            acc[:, tc2 * D + dh * 512:tc2 * D + (dh + 1) * 512],
                            dbank[:], gates[:, tc2 * NE + e_i:tc2 * NE + e_i + 1],
                            acc[:, tc2 * D + dh * 512:tc2 * D + (dh + 1) * 512],
                            ALU.mult, ALU.add)
                wg_free[par] = S.cnt["sp"]
                it += 1

        if SPARSE and "B" in PHASES:
            NTT2 = CAP // 256
            sc_total = S.cnt.get("sg", 0)
            wait("gpsimd", "sg", sc_total)
            it = 0
            for e_i in range(NE):
                ep = e_i % 2
                xsem = "sxl" if ep == 0 else "sxm"
                if e_i == 0:
                    op("gpsimd", "dma_start",
                       out=xgl[0][:].rearrange("p (b d) -> p b d", b=NBL),
                       in_=xg[0:CAP, :].rearrange("(b p) d -> p b d", p=128),
                       inc="sxl", inc_n=16)
                if e_i + 1 < NE:
                    op("gpsimd", "dma_start",
                       out=xgl[1 - ep][:].rearrange("p (b d) -> p b d", b=NBL),
                       in_=xg[(e_i + 1) * CAP:(e_i + 2) * CAP, :].rearrange(
                           "(b p) d -> p b d", p=128),
                       inc=("sxm" if ep == 0 else "sxl"), inc_n=16)
                wait("tensor", xsem)
                for b in range(NBL):
                    for k in range(NK):
                        op("tensor", "transpose", psT[:, k * 128:(k + 1) * 128],
                           in_=xgl[ep][:, b * D + k * 128:b * D + (k + 1) * 128],
                           identity=identB[:], inc=("sp" if k == NK - 1 else None))
                    vop("tensor_copy",
                        xgT[:].rearrange("p (k t) -> p k t", k=NK)[
                            :, :, b * 128:(b + 1) * 128],
                        psT[:].rearrange("p (k t) -> p k t", k=NK))

                for fb in range(NFB):
                    par = it % 2
                    wait("gpsimd", "sp", wg_free[par])
                    wait("gpsimd", "sw")
                    for k in range(NK):
                        op("gpsimd", "dma_start",
                           out=WguB[par][:, k * 1024:(k + 1) * 1024].rearrange(
                               "p (h c) -> p h c", h=2),
                           in_=wgu5[e_i, fb, k * 128:(k + 1) * 128, :, :],
                           inc="sw", inc_n=16)
                    op("gpsimd", "dma_start",
                       out=WdnB[par][:].rearrange("p (f d) -> p f d", f=NFC),
                       in_=wdn4[e_i, :, fb * NFC:(fb + 1) * NFC, :],
                       inc="sw", inc_n=16)
                    w_cnt = S.cnt["sw"]
                    wait("tensor", "sw", w_cnt)

                    for tt in range(NTT2):
                        for fc in range(NFC):
                            bank = psA if fc % 2 == 0 else psB
                            for k in range(NK):
                                op("tensor", "matmul", bank[:, 0:256],
                                   lhsT=WguB[par][:, k * 1024 + fc * 128:
                                                  k * 1024 + (fc + 1) * 128],
                                   rhs=xgT[:, k * CAP + tt * 256:
                                           k * CAP + (tt + 1) * 256],
                                   start=(k == 0), stop=(k == NK - 1),
                                   inc=("sp" if k == NK - 1 else None))
                            for k in range(NK):
                                op("tensor", "matmul", bank[:, 256:512],
                                   lhsT=WguB[par][:, k * 1024 + 512 + fc * 128:
                                                  k * 1024 + 512 + (fc + 1) * 128],
                                   rhs=xgT[:, k * CAP + tt * 256:
                                           k * CAP + (tt + 1) * 256],
                                   start=(k == 0), stop=(k == NK - 1),
                                   inc=("sp" if k == NK - 1 else None))
                            sg_ = sgb[fc % 2]
                            aop("activation", sg_[:, 0:256], bank[:, 0:256], AF.Silu)
                            vop("tensor_mul", actb[fc][:, 0:256],
                                sg_[:, 0:256], bank[:, 256:512])
                        dbanks = (psC, psD, psW, psV)
                        for gi in range(4):
                            tok, dh = gi // 2, gi % 2
                            dbank = dbanks[gi]
                            ysl = yacc[gi][:, tt * 512:(tt + 1) * 512]
                            for fc in range(NFC):
                                last = (fc == NFC - 1) and fb == 0
                                op("tensor", "matmul", dbank[:],
                                   lhsT=actb[fc][:, tok * 128:(tok + 1) * 128],
                                   rhs=WdnB[par][:, fc * 1024 + dh * 512:
                                                 fc * 1024 + (dh + 1) * 512],
                                   start=(fc == 0), stop=last,
                                   inc=("sp" if last else None))
                            if fb > 0:
                                op("tensor", "matmul", dbank[:], lhsT=identB[:],
                                   rhs=ysl, start=False, stop=True, inc="sp")
                            if gi % 2 == 0:
                                vop("tensor_copy", ysl, dbank[:])
                            else:
                                aop("copy", ysl, dbank[:])
                    wg_free[par] = S.cnt["sp"]
                    it += 1
                for gi2 in range(4):
                    tok2, dh2 = gi2 // 2, gi2 % 2
                    op("gpsimd", "dma_start",
                       out=yout[e_i * CAP:(e_i + 1) * CAP, :].rearrange(
                           "(t b p) d -> p t b d", p=128, b=2)[
                           :, :, tok2, dh2 * 512:(dh2 + 1) * 512],
                       in_=yacc[gi2][:].rearrange("p (t d) -> p t d", d=512),
                       inc="sy", inc_n=16)
                for gi2 in range(4):
                    S._mark(S.readers, yacc[gi2].name, "sy", S.cnt["sy"])
            st_total = S.cnt["sy"]

            wait("gpsimd", "sy", st_total)
            wait("sync", "sy", st_total)
            gbuf = ([xgl[0][:, i * D:(i + 1) * D] for i in range(3)] +
                    [xgl[1][:, i * D:(i + 1) * D] for i in range(3)])
            outv = out.rearrange("(i p) d -> p i d", p=128)
            accv = acc[:].rearrange("p (i d) -> p i d", d=D)

            def g_issue(ix):
                slot = ix % 6
                op("gpsimd", "indirect_dma_start",
                   out=gbuf[slot], out_offset=None, in_=yout,
                   in_offset=IndirectOffsetOnAxis(idxs[:, ix:ix + 1], 0),
                   inc=f"gb{slot}", inc_n=16)

            def g_acc(ix):
                slot = ix % 6
                tc, r = ix // 2, ix % 2
                wait("vector", f"gb{slot}")
                vop("scalar_tensor_tensor", acc[:, tc * D:(tc + 1) * D],
                    gbuf[slot], groutes[:, ix:ix + 1],
                    acc[:, tc * D:(tc + 1) * D], ALU.mult, ALU.add)
                if r == 1:
                    op("sync", "dma_start", out=outv[:, tc:tc + 1, :],
                       in_=accv[:, tc:tc + 1, :], inc="so", inc_n=16)

            NN = 2 * n_own
            for i in range(NN + 4):
                if i < NN:
                    g_issue(i)
                if i >= 4:
                    g_acc(i - 4)

        if not (SPARSE and "B" in PHASES):
            wait("sync", "sv")
            op("sync", "dma_start", out=out.rearrange("(i p) d -> p i d", p=128),
               in_=acc[:].rearrange("p (i d) -> p i d", d=D), inc="so", inc_n=16)
        wait("sync", "so")
        wait("gpsimd", "so")

        @block.sync
        def _(e):
            for fn in S.q["sync"]:
                fn(e)

        @block.gpsimd
        def _(e):
            for fn in S.q["gpsimd"]:
                fn(e)

        @block.tensor
        def _(e):
            for fn in S.q["tensor"]:
                fn(e)

        @block.vector
        def _(e):
            for fn in S.q["vector"]:
                fn(e)

        @block.scalar
        def _(e):
            for fn in S.q["scalar"]:
                fn(e)

    return nc


_NC_CACHE = None


def kernel(L0_attn_x, L0_attn_Wq, L0_attn_Wk, L0_attn_Wv, L0_attn_Wbeta,
           L0_attn_b_beta, L0_attn_Wout, L0_ffn_w_gate, L0_ffn_w_gate_up,
           L0_ffn_w_down):
    global _NC_CACHE
    x = np.ascontiguousarray(np.asarray(L0_attn_x, dtype=np.float32))
    common = dict(
        wq=np.ascontiguousarray(L0_attn_Wq, dtype=np.float32),
        wk=np.ascontiguousarray(L0_attn_Wk, dtype=np.float32),
        wv=np.ascontiguousarray(L0_attn_Wv, dtype=np.float32),
        wout=np.ascontiguousarray(L0_attn_Wout, dtype=np.float32),
        wbeta=np.ascontiguousarray(L0_attn_Wbeta, dtype=np.float32),
        bbeta=np.ascontiguousarray(L0_attn_b_beta, dtype=np.float32),
        wgate=np.ascontiguousarray(L0_ffn_w_gate, dtype=np.float32),
        wgu=np.ascontiguousarray(L0_ffn_w_gate_up, dtype=np.float32),
        wdn=np.ascontiguousarray(L0_ffn_w_down, dtype=np.float32),
        cident=np.eye(128, dtype=np.float32),
        cexp=np.tile((np.arange(NE) * CAP).astype(np.float32), (128, 1)),
        cslmask=np.tril(np.ones((128, 128), np.float32), -1),
        csumask=np.triu(np.ones((128, 128), np.float32), 0),
    )
    in_maps = []
    for c in range(8):
        b, h = c >> 1, c & 1
        if h == 0:
            xs = np.concatenate([np.zeros((2048, D), np.float32), x[b, 0:2048]], 0)
        else:
            xs = x[b]
        m = dict(common)
        m["xpad"] = np.ascontiguousarray(xs)
        in_maps.append(m)

    if _NC_CACHE is None:
        _NC_CACHE = build_program()
    res = run_bass_kernel_spmd(_NC_CACHE, in_maps, core_ids=list(range(8)))
    outp = np.empty((4, 4096, D), dtype=np.float32)
    for c in range(8):
        b, h = c >> 1, c & 1
        outp[b, h * 2048:(h + 1) * 2048, :] = res.results[c]["out"]
    return outp
